# version 02 (final)
# baseline (speedup 1.0000x reference)
import numpy as np
import contextlib
import concourse.bass as bass
import concourse.mybir as mybir

F32 = mybir.dt.float32
BF16 = mybir.dt.bfloat16
I32 = mybir.dt.int32
U32 = mybir.dt.uint32
AF = mybir.ActivationFunctionType
ALU = mybir.AluOpType
AX = mybir.AxisListType

ENGINES = ("sync", "tensor", "vector", "scalar", "gpsimd")
T = 4096
D = 1024
NT = T // 128
KC = D // 128
NB = T // 512


class Prog:
    def __init__(self, nc):
        self.nc = nc
        self.es = contextlib.ExitStack()
        self.stages = []
        self._cur = None
        self._sbuf_bytes = 0
        self.extra_sems = {}
        self.gp_reg_requests = {}
        self.gp_regs = {}

    def sbuf(self, name, shape, dtype=F32, side=None):
        self._sbuf_bytes += int(np.prod(shape)) * {F32: 4, BF16: 2, I32: 4, U32: 4}[dtype]
        return self.es.enter_context(self.nc.sbuf_tensor(name, list(shape), dtype, side=side))

    def psum(self, name, shape=(128, 512), dtype=F32):
        return self.es.enter_context(self.nc.psum_tensor(name, list(shape), dtype))

    def sem(self, name):
        s = self.es.enter_context(self.nc.semaphore(name))
        self.extra_sems[name] = s
        return s

    def dram(self, name, shape, dtype=F32, kind=None, addr_space="Local"):
        if kind:
            return self.nc.dram_tensor(name, list(shape), dtype, kind=kind, addr_space=addr_space)
        return self.nc.dram_tensor(name, list(shape), dtype, addr_space=addr_space)

    def stage(self, name=""):
        self._cur = {e: [] for e in ENGINES}
        self._cur["_name"] = name
        self.stages.append(self._cur)
        return self._cur

    def comp(self, engine, fn, n=1):
        self._cur[engine].append((fn, n, 0))

    def dma(self, engine, fn):
        self._cur[engine].append((fn, 0, 1))

    def emit(self):
        nc = self.nc
        BC = self.sem("BC")
        BD = self.sem("BD")
        BG = self.sem("BG")
        bc_tot = bd_tot = bg_tot = 0
        plans = {e: [] for e in ENGINES}
        for st in self.stages:
            sbc = sbd = sbg = 0
            for e in ENGINES:
                ops = st[e]
                if not ops:
                    continue
                n_comp = sum(o[1] for o in ops)
                n_dma = sum(o[2] for o in ops)
                plans[e].append((bc_tot, bd_tot, bg_tot, ops, n_comp, e))
                sbc += n_comp
                if e == "gpsimd":
                    sbg += n_dma
                else:
                    sbd += n_dma
            bc_tot += sbc
            bd_tot += 16 * sbd
            bg_tot += 16 * sbg
        self.final_bc, self.final_bd, self.final_bg = bc_tot, bd_tot, bg_tot

        with self.es, nc.Block() as block:
            for ename in ENGINES:
                plan = plans[ename]
                is_final = ename == "sync"
                if not plan and not is_final:
                    continue

                def body(eng, plan=plan, is_final=is_final, ename=ename):
                    if ename == "gpsimd":
                        for nm, val in self.gp_reg_requests.items():
                            r = eng.alloc_register(nm)
                            eng.reg_mov(r, val)
                            self.gp_regs[nm] = r
                    last_bc = last_bd = last_bg = -1
                    for (wbc, wbd, wbg, ops, n_comp, e) in plan:
                        if wbc > 0 and wbc != last_bc:
                            eng.wait_ge(BC, wbc)
                            last_bc = wbc
                        if wbd > 0 and wbd != last_bd:
                            eng.wait_ge(BD, wbd)
                            last_bd = wbd
                        if wbg > 0 and wbg != last_bg:
                            eng.wait_ge(BG, wbg)
                            last_bg = wbg
                        left = n_comp
                        for (fn, nc_i, nd_i) in ops:
                            try:
                                inst = fn(eng)
                            except Exception as _ex:
                                raise RuntimeError(f"emit failed in stage (engine={e})") from _ex
                            if nd_i:
                                inst.then_inc(BG if e == "gpsimd" else BD, 16)
                            else:
                                left -= nc_i
                                if left == 0 and n_comp > 0:
                                    inst.then_inc(BC, n_comp)
                    if is_final:
                        if self.final_bc > last_bc:
                            eng.wait_ge(BC, self.final_bc)
                        if self.final_bd > 0 and self.final_bd > last_bd:
                            eng.wait_ge(BD, self.final_bd)
                        if self.final_bg > 0 and self.final_bg > last_bg:
                            eng.wait_ge(BG, self.final_bg)

                getattr(block, ename)(body)
        return nc



def build_consts(p: Prog):
    c = {}
    c["idb"] = p.sbuf("c_idb", [128, 128], BF16, side="right")
    c["idf"] = p.sbuf("c_idf", [128, 128], F32, side="right")
    c["iota_col"] = p.sbuf("c_iotacol", [128, 1], F32, side="right")
    c["iota_row"] = p.sbuf("c_iotarow", [128, 128], F32, side="right")
    c["ones_colb"] = p.sbuf("c_onescolb", [128, 1], BF16, side="right")
    c["ones_colf"] = p.sbuf("c_onescolf", [128, 1], F32, side="right")
    c["ones_rowb"] = p.sbuf("c_onesrowb", [1, 512], BF16, side="right")
    c["ones_rowf"] = p.sbuf("c_onesrowf", [1, 128], F32, side="right")
    c["eps_ln"] = p.sbuf("c_epsln", [128, 1], F32, side="right")
    p.stage("consts0")
    p.comp("gpsimd", lambda g, c=c: g.iota(c["iota_row"][:], [[1, 128]], channel_multiplier=0,
                                           allow_small_or_imprecise_dtypes=True))
    p.comp("gpsimd", lambda g, c=c: g.iota(c["iota_col"][:], [[1, 1]], channel_multiplier=1,
                                           allow_small_or_imprecise_dtypes=True))
    p.comp("gpsimd", lambda g, c=c: g.memset(c["ones_colb"][:], 1.0))
    p.comp("gpsimd", lambda g, c=c: g.memset(c["ones_colf"][:], 1.0))
    p.comp("gpsimd", lambda g, c=c: g.memset(c["ones_rowb"][:], 1.0))
    p.comp("gpsimd", lambda g, c=c: g.memset(c["ones_rowf"][:], 1.0))
    p.comp("gpsimd", lambda g, c=c: g.memset(c["eps_ln"][:], 1e-5))
    p.stage("consts1")
    p.comp("vector", lambda v, c=c: v.tensor_scalar(c["idf"][:], c["iota_row"][:], c["iota_col"][:],
                                                    None, ALU.is_equal))
    p.stage("consts2")
    p.comp("vector", lambda v, c=c: v.tensor_copy(c["idb"][:], c["idf"][:]))
    return c



def load_weight_bf16(p, name_sb, w_dram_ap, din, dout, engine="gpsimd"):
    sb = p.sbuf(name_sb, [128, din // 128, dout], BF16)
    src = w_dram_ap.rearrange("(k p) o -> p k o", p=128)
    p.dma(engine, lambda g, sb=sb, src=src: g.dma_start(out=sb[:], in_=src))
    return sb


def load_bias_cols(p, name_sb, b_dram_ap, dout, engine="sync"):
    sb, f = sbuf_tmp(p, name_sb, [128, dout // 128], F32, side="right")
    src = b_dram_ap.rearrange("(k p) -> p k", p=128)
    p.dma(engine, lambda g, sb=sb, src=src: g.dma_start(out=sb[:], in_=src, allow_slow_non_contiguous=True))
    return sb, f


def bias_bcast_row(p, c, name, b_dram_ap, dout, pb, dtype=F32):
    out, fo = sbuf_tmp(p, name, [128, dout], dtype, side="right")
    row, fr = sbuf_tmp(p, name + "_row", [1, dout], F32, side="right")
    p.stage(name + "_ld")
    p.dma("sync", lambda g, row=row, b=b_dram_ap: g.dma_start(out=row[:], in_=b.rearrange("(a d) -> a d", a=1)))
    onesf_row = c["ones_rowf"]
    p.comp("gpsimd", lambda g, o=onesf_row: g.memset(o[:], 1.0))
    for j in range(dout // 512):
        p.stage(name + f"_mm{j}")
        p.comp("tensor", lambda t, row=row, o=onesf_row, pb=pb, j=j:
               t.matmul(pb[:, 0:512], o[0:1, :], row[:, j * 512:(j + 1) * 512], start=True, stop=True))
        p.stage(name + f"_ev{j}")
        p.comp("vector", lambda v, out=out, pb=pb, j=j:
               v.tensor_copy(out[:, j * 512:(j + 1) * 512], pb[:, 0:512]))
    def free_all():
        fr(); fo()
    return out, free_all



def sbuf_tmp(p, name, shape, dtype=F32, side=None):
    cm = p.nc.sbuf_tensor(name, list(shape), dtype, side=side)
    h = cm.__enter__()
    return h, (lambda: cm.__exit__(None, None, None))



def prologue_ln(p, c, x_dram_ap, lnT, pbt0, pbt1, ln_tok_dram=None, uniq="", post_hook=None):
    G = 8
    NG = NT // G
    _frees = []
    def _tmp(nm, sh, dt=F32):
        h, f = sbuf_tmp(p, nm + uniq, sh, dt)
        _frees.append(f)
        return h
    xa = _tmp("pl_xa", [128, G, D], F32)
    xb = _tmp("pl_xb", [128, G, D], F32)
    sq = _tmp("pl_sq", [128, D], F32)
    sums = _tmp("pl_sums", [128, NT], F32)
    sumsq = _tmp("pl_sumsq", [128, NT], F32)
    mu = _tmp("pl_mu", [128, NT], F32)
    var = _tmp("pl_var", [128, NT], F32)
    rstd = _tmp("pl_rstd", [128, NT], F32)
    sd = _tmp("pl_sd", [128, NT], F32)
    lt = _tmp("pl_lt", [128, G, D], BF16)

    def xbuf(g):
        return xa if g % 2 == 0 else xb

    xsrc = x_dram_ap.rearrange("(n p) d -> n p d", p=128)

    p.stage("pl_load0")
    p.dma("sync", lambda s: s.dma_start(out=xa[:], in_=xsrc[0:G].rearrange("n p d -> p n d")))

    for g in range(NG):
        xg = xbuf(g)
        p.stage(f"pl_stats{g}")
        if g + 1 < NG:
            nxt = xbuf(g + 1)
            p.dma("sync", lambda s, g=g, nxt=nxt: s.dma_start(
                out=nxt[:], in_=xsrc[(g + 1) * G:(g + 2) * G].rearrange("n p d -> p n d")))
        for j in range(G):
            ti = g * G + j
            p.comp("scalar", lambda s, xg=xg, j=j, ti=ti: s.activation(
                sq[:], xg[:, j], AF.Square, accum_out=sumsq[:, ti:ti + 1]))
            p.comp("vector", lambda v, xg=xg, j=j, ti=ti: v.reduce_sum(
                sums[:, ti:ti + 1], xg[:, j], axis=AX.X))
        sl = slice(g * G, (g + 1) * G)
        p.stage(f"pl_statm{g}")
        def statm(v, sl=sl):
            v.tensor_single_scalar(mu[:, sl], sums[:, sl], 1.0 / D, ALU.mult)
            return v.tensor_single_scalar(var[:, sl], sumsq[:, sl], 1.0 / D, ALU.mult)
        p.comp("vector", statm, n=2)
        p.stage(f"pl_statm2_{g}")
        p.comp("vector", lambda v, sl=sl: v.tensor_tensor(sq[:, 0:G], mu[:, sl], mu[:, sl], ALU.mult))
        p.stage(f"pl_statm3_{g}")
        p.comp("vector", lambda v, sl=sl: v.scalar_tensor_tensor(
            var[:, sl], sq[:, 0:G], -1.0, var[:, sl], ALU.mult, ALU.add))
        p.stage(f"pl_sqrt{g}")
        p.comp("scalar", lambda s, sl=sl: s.activation(sd[:, sl], var[:, sl], AF.Sqrt,
                                                       bias=c["eps_ln"][:, 0:1]))
        p.stage(f"pl_recip{g}")
        p.comp("vector", lambda v, sl=sl: v.reciprocal(rstd[:, sl], sd[:, sl]))
        p.stage(f"pl_apply{g}")
        def apply(v, g=g, xg=xg):
            last = None
            for j in range(G):
                ti = g * G + j
                last = v.tensor_scalar(lt[:, j], xg[:, j], mu[:, ti:ti + 1],
                                       rstd[:, ti:ti + 1], ALU.subtract, ALU.mult)
            return last
        p.comp("vector", apply, n=G)
        if post_hook is not None:
            post_hook(p, lt, g, G)
        if ln_tok_dram is not None:
            p.stage(f"pl_tokout{g}")
            dst = ln_tok_dram.rearrange("(n p) d -> p n d", p=128)[:, g * G:(g + 1) * G]
            p.dma("sync", lambda s, dst=dst, lt=lt: s.dma_start(out=dst, in_=lt[:]))
        for jj in range(0, G, 2):
            p.stage(f"pl_tr{g}_{jj}")
            for (j, pb) in ((jj, pbt0), (jj + 1, pbt1)):
                def trs(t, j=j, pb=pb, lt=lt):
                    last = None
                    for k in range(KC):
                        last = t.transpose(pb[:, k * 128:(k + 1) * 128],
                                           lt[:, j, k * 128:(k + 1) * 128], c["idb"][:])
                    return last
                p.comp("tensor", trs, n=KC)
            p.stage(f"pl_trev{g}_{jj}")
            for (j, pb, eng) in ((jj, pbt0, "vector"), (jj + 1, pbt1, "scalar")):
                ti = g * G + j
                if eng == "vector":
                    def ev(e, pb=pb, ti=ti):
                        return e.tensor_copy(lnT[:, :, ti * 128:(ti + 1) * 128],
                                             pb[:].rearrange("p (k f) -> p k f", f=128))
                else:
                    def ev(e, pb=pb, ti=ti):
                        return e.copy(lnT[:, :, ti * 128:(ti + 1) * 128],
                                      pb[:].rearrange("p (k f) -> p k f", f=128))
                p.comp(eng, ev)
    for f in reversed(_frees):
        f()
    return None



def proj_fm(p, w_sb, rhsT, outT, dout, din=D, bias_cols=None, act=None, scale=1.0,
            pbanks=None, evac_dtypes=None, name="pf", tblocks=None, rhs_off=0, out_off=0):
    MC = dout // 128
    KN = din // 128
    tb = tblocks if tblocks is not None else range(NB)
    pairs = [(m, nb) for m in range(MC) for nb in tb]
    npar = 2
    for i0 in range(0, len(pairs), npar):
        cur = pairs[i0:i0 + npar]
        prev = pairs[i0 - npar:i0] if i0 > 0 else []
        p.stage(f"{name}_mm{i0}")
        for s_i, (m, nb) in enumerate(cur):
            pb = pbanks[(i0 // npar) % 2 * npar + s_i]
            def mms(t, m=m, nb=nb, pb=pb):
                last = None
                for k in range(KN):
                    last = t.matmul(pb[:, 0:512], w_sb[:, k, m * 128:(m + 1) * 128],
                                    rhsT[:, k, nb * 512 - rhs_off:(nb + 1) * 512 - rhs_off],
                                    start=(k == 0), stop=(k == KN - 1))
                return last
            p.comp("tensor", mms, n=KN)
        for s_i, (m, nb) in enumerate(prev):
            pb = pbanks[((i0 - npar) // npar) % 2 * npar + s_i]
            _emit_evac(p, pb, outT, m, nb, bias_cols, act, scale, s_i, out_off)
    i0 = len(pairs)
    p.stage(f"{name}_mmtail")
    prev = pairs[i0 - npar:i0]
    for s_i, (m, nb) in enumerate(prev):
        pb = pbanks[((i0 - npar) // npar) % 2 * npar + s_i]
        _emit_evac(p, pb, outT, m, nb, bias_cols, act, scale, s_i, out_off)


def _emit_evac(p, pb, outT, m, nb, bias_cols, act, scale, s_i, out_off=0):
    dst = outT[:, m, nb * 512 - out_off:(nb + 1) * 512 - out_off]
    if act is not None or bias_cols is not None or scale != 1.0:
        fn = act if act is not None else AF.Identity
        if bias_cols is not None:
            p.comp("scalar", lambda s, dst=dst, pb=pb, fn=fn, m=m, scale=scale:
                   s.activation(dst, pb[:, 0:512], fn, bias=bias_cols[:, m:m + 1], scale=scale))
        else:
            p.comp("scalar", lambda s, dst=dst, pb=pb, fn=fn, scale=scale:
                   s.activation(dst, pb[:, 0:512], fn, scale=scale))
    else:
        eng = "vector" if s_i % 2 == 0 else "scalar"
        if eng == "vector":
            p.comp("vector", lambda v, dst=dst, pb=pb: v.tensor_copy(dst, pb[:, 0:512]))
        else:
            p.comp("scalar", lambda s, dst=dst, pb=pb: s.copy(dst, pb[:, 0:512]))



def epilogue_ln_add(p, c, y_dram_ap, xin_dram_ap, xout_dram_ap, uniq=""):
    G = 8
    NG = NT // G
    ya, fya = sbuf_tmp(p, "ep_ya" + uniq, [128, G, D], F32)
    yb, fyb = sbuf_tmp(p, "ep_yb" + uniq, [128, G, D], F32)
    xa, fxa = sbuf_tmp(p, "ep_xa" + uniq, [128, G, D], F32)
    xb, fxb = sbuf_tmp(p, "ep_xb" + uniq, [128, G, D], F32)
    ot, fot = sbuf_tmp(p, "ep_ot" + uniq, [128, G, D], F32)
    sq, fsq = sbuf_tmp(p, "ep_sq" + uniq, [128, D], F32)
    sums, f1 = sbuf_tmp(p, "ep_sums" + uniq, [128, NT], F32)
    sumsq, f2 = sbuf_tmp(p, "ep_sumsq" + uniq, [128, NT], F32)
    mu, f3 = sbuf_tmp(p, "ep_mu" + uniq, [128, NT], F32)
    var, f4 = sbuf_tmp(p, "ep_var" + uniq, [128, NT], F32)
    rstd, f5 = sbuf_tmp(p, "ep_rstd" + uniq, [128, NT], F32)
    sd, f6 = sbuf_tmp(p, "ep_sd" + uniq, [128, NT], F32)

    ysrc = y_dram_ap.rearrange("(n p) d -> p n d", p=128)
    xsrc = xin_dram_ap.rearrange("(n p) d -> p n d", p=128)
    xdst = xout_dram_ap.rearrange("(n p) d -> p n d", p=128)

    def buf(b0, b1, g):
        return b0 if g % 2 == 0 else b1

    p.stage("ep_load0" + uniq)
    p.dma("sync", lambda s: s.dma_start(out=ya[:], in_=ysrc[:, 0:G]))
    p.dma("sync", lambda s: s.dma_start(out=xa[:], in_=xsrc[:, 0:G]))
    for g in range(NG):
        yg = buf(ya, yb, g)
        xg = buf(xa, xb, g)
        sl = slice(g * G, (g + 1) * G)
        p.stage(f"ep_stats{g}" + uniq)
        if g + 1 < NG:
            nyt = buf(ya, yb, g + 1)
            nxt = buf(xa, xb, g + 1)
            p.dma("sync", lambda s, g=g, nyt=nyt: s.dma_start(
                out=nyt[:], in_=ysrc[:, (g + 1) * G:(g + 2) * G]))
            p.dma("sync", lambda s, g=g, nxt=nxt: s.dma_start(
                out=nxt[:], in_=xsrc[:, (g + 1) * G:(g + 2) * G]))
        for j in range(G):
            ti = g * G + j
            p.comp("scalar", lambda s, yg=yg, j=j, ti=ti: s.activation(
                sq[:], yg[:, j], AF.Square, accum_out=sumsq[:, ti:ti + 1]))
            p.comp("vector", lambda v, yg=yg, j=j, ti=ti: v.reduce_sum(
                sums[:, ti:ti + 1], yg[:, j], axis=AX.X))
        p.stage(f"ep_statm{g}" + uniq)
        def statm(v, sl=sl):
            v.tensor_single_scalar(mu[:, sl], sums[:, sl], 1.0 / D, ALU.mult)
            return v.tensor_single_scalar(var[:, sl], sumsq[:, sl], 1.0 / D, ALU.mult)
        p.comp("vector", statm, n=2)
        p.stage(f"ep_statm2_{g}" + uniq)
        p.comp("vector", lambda v, sl=sl: v.tensor_tensor(sq[:, 0:G], mu[:, sl], mu[:, sl], ALU.mult))
        p.stage(f"ep_statm3_{g}" + uniq)
        p.comp("vector", lambda v, sl=sl: v.scalar_tensor_tensor(
            var[:, sl], sq[:, 0:G], -1.0, var[:, sl], ALU.mult, ALU.add))
        p.stage(f"ep_sqrt{g}" + uniq)
        p.comp("scalar", lambda s, sl=sl: s.activation(sd[:, sl], var[:, sl], AF.Sqrt,
                                                       bias=c["eps_ln"][:, 0:1]))
        p.stage(f"ep_recip{g}" + uniq)
        p.comp("vector", lambda v, sl=sl: v.reciprocal(rstd[:, sl], sd[:, sl]))
        p.stage(f"ep_apply{g}" + uniq)
        def apply(v, g=g, yg=yg):
            last = None
            for j in range(G):
                ti = g * G + j
                last = v.tensor_scalar(yg[:, j], yg[:, j], mu[:, ti:ti + 1],
                                       rstd[:, ti:ti + 1], ALU.subtract, ALU.mult)
            return last
        p.comp("vector", apply, n=G)
        p.stage(f"ep_add{g}" + uniq)
        def addx(v, g=g, yg=yg, xg=xg):
            last = None
            for j in range(G):
                last = v.tensor_tensor(ot[:, j], xg[:, j], yg[:, j], ALU.add)
            return last
        p.comp("vector", addx, n=G)
        p.stage(f"ep_out{g}" + uniq)
        p.dma("sync", lambda s, g=g: s.dma_start(out=xdst[:, g * G:(g + 1) * G], in_=ot[:]))
    for f in (f6, f5, f4, f3, f2, f1, fsq, fot, fxb, fxa, fyb, fya):
        f()


def transpose_fm_to_tok(p, c, srcT_bf16, dst_dram_f32, pbt0, pbt1, kc=KC, nt=NT, uniq=""):
    ot, fot = sbuf_tmp(p, "tft_ot" + uniq, [128, 2, kc * 128], F32)
    dst = dst_dram_f32.rearrange("(n p) d -> p n d", p=128)
    for jj in range(0, nt, 2):
        p.stage(f"tft_tr{jj}" + uniq)
        for (j, pbx) in ((jj, pbt0), (jj + 1, pbt1)):
            def trs(t, j=j, pbx=pbx):
                last = None
                for k in range(kc):
                    last = t.transpose(pbx[:, k * 128:(k + 1) * 128],
                                       srcT_bf16[:, k, j * 128:(j + 1) * 128], c["idb"][:])
                return last
            p.comp("tensor", trs, n=kc)
        p.stage(f"tft_ev{jj}" + uniq)
        p.comp("vector", lambda v, jj=jj: v.tensor_copy(ot[:, 0], pbt0[:]))
        p.comp("scalar", lambda s, jj=jj: s.copy(ot[:, 1], pbt1[:]))
        p.stage(f"tft_st{jj}" + uniq)
        p.dma("sync", lambda s, jj=jj: s.dma_start(out=dst[:, jj:jj + 2], in_=ot[:]))
    fot()


_doc_L0A = """L0_attn: segmented attention (SEG=512) + dilated cross-segment (stride 4,
offsets +-1,+-2, weight 0.25 each, separate softmaxes)."""
import numpy as np



SEQ = 2048
NS = 4
H = 16
DK = 64


def build_l0_attn(p, c, xin_ap, xout_ap, w, pb, pbt0, pbt1, uniq="l0a"):
    y_d = p.dram(uniq + "_y", [T, D], F32)
    aA_d = p.dram(uniq + "_aA", [64, KC, T], BF16)
    aB_d = p.dram(uniq + "_aB", [64, KC, T], BF16)

    lnT, f_lnT = sbuf_tmp(p, uniq + "_lnT", [128, KC, T], BF16)
    prologue_ln(p, c, xin_ap, lnT, pbt0, pbt1, uniq=uniq)

    p.stage(uniq + "_bld")
    bq, f_bq = load_bias_cols(p, uniq + "_bq", w["bq"], D)
    bk, f_bk = load_bias_cols(p, uniq + "_bk", w["bk"], D)
    bv, f_bv = load_bias_cols(p, uniq + "_bv", w["bv"], D)

    q_d = p.dram(uniq + "_q", [128, KC, T], BF16)
    k_d = p.dram(uniq + "_k", [128, KC, T], BF16)
    v_d = p.dram(uniq + "_v", [128, KC, T], BF16)
    pbuf, f_pbuf = sbuf_tmp(p, uniq + "_pbuf", [128, KC, SEQ], BF16)
    for b in range(2):
        tb = [b * 4 + i for i in range(4)]
        for (wkey, bcol, dst) in (("Wq", bq, q_d), ("Wk", bk, k_d), ("Wv", bv, v_d)):
            p.stage(uniq + f"_w{wkey}{b}")
            wsb, f_wsb = sbuf_tmp(p, uniq + f"_w{wkey}{b}", [128, KC, D], BF16)
            p.dma("gpsimd", lambda g, wsb=wsb, wkey=wkey: g.dma_start(
                out=wsb[:], in_=w[wkey].rearrange("(k p) o -> p k o", p=128)))
            proj_fm(p, wsb, lnT, pbuf, D, bias_cols=bcol, pbanks=pb[0:4],
                    name=uniq + f"_{wkey}{b}", tblocks=tb, out_off=b * SEQ)
            p.stage(uniq + f"_st{wkey}{b}")
            p.dma("sync", lambda s, dst=dst, b=b: s.dma_start(
                out=dst.ap()[:, :, b * SEQ:(b + 1) * SEQ], in_=pbuf[:]))
            f_wsb()
    f_pbuf()
    f_lnT()

    for b in range(2):
        qT, f_qT = sbuf_tmp(p, uniq + f"_qT{b}", [128, KC, SEQ], BF16)
        kT, f_kT = sbuf_tmp(p, uniq + f"_kT{b}", [128, KC, SEQ], BF16)
        v_tok, f_vt = sbuf_tmp(p, uniq + f"_vtok{b}", [128, 16, D], BF16)
        vd_tok, f_vdt = sbuf_tmp(p, uniq + f"_vdtok{b}", [128, NS, D], BF16)
        vT, f_vT = sbuf_tmp(p, uniq + f"_vT{b}", [128, KC, SEQ], BF16)
        p.stage(uniq + f"_ld{b}")
        p.dma("sync", lambda s, b=b, qT=qT: s.dma_start(out=qT[:], in_=q_d.ap()[:, :, b * SEQ:(b + 1) * SEQ]))
        p.dma("sync", lambda s, b=b, kT=kT: s.dma_start(out=kT[:], in_=k_d.ap()[:, :, b * SEQ:(b + 1) * SEQ]))
        p.dma("sync", lambda s, b=b, vT=vT: s.dma_start(out=vT[:], in_=v_d.ap()[:, :, b * SEQ:(b + 1) * SEQ]))

        for jj in range(0, 16, 2):
            p.stage(uniq + f"_vtr{b}_{jj}")
            for (j, pbt) in ((jj, pbt0), (jj + 1, pbt1)):
                def trs(t, j=j, pbt=pbt, vT=vT):
                    last = None
                    for k2 in range(KC):
                        last = t.transpose(pbt[:, k2 * 128:(k2 + 1) * 128],
                                           vT[:, k2, j * 128:(j + 1) * 128], c["idb"][:])
                    return last
                p.comp("tensor", trs, n=KC)
            p.stage(uniq + f"_vte{b}_{jj}")
            p.comp("vector", lambda v, jj=jj, v_tok=v_tok: v.tensor_copy(v_tok[:, jj, :], pbt0[:]))
            p.comp("scalar", lambda s, jj=jj, v_tok=v_tok: s.copy(v_tok[:, jj + 1, :], pbt1[:]))
        for s_ in range(0, NS, 2):
            p.stage(uniq + f"_vdtr{b}_{s_}")
            for (j, pbt) in ((s_, pbt0), (s_ + 1, pbt1)):
                def trsd(t, j=j, pbt=pbt, vT=vT):
                    last = None
                    for k2 in range(KC):
                        last = t.transpose(pbt[:, k2 * 128:(k2 + 1) * 128],
                                           vT[:, k2, j * 512:(j + 1) * 512:4], c["idb"][:])
                    return last
                p.comp("tensor", trsd, n=KC)
            p.stage(uniq + f"_vdte{b}_{s_}")
            p.comp("vector", lambda v, s_=s_, vd_tok=vd_tok: v.tensor_copy(vd_tok[:, s_, :], pbt0[:]))
            p.comp("scalar", lambda s, s_=s_, vd_tok=vd_tok: s.copy(vd_tok[:, s_ + 1, :], pbt1[:]))
        f_vT()

        EA, f_EA = sbuf_tmp(p, uniq + f"_EA{b}", [128, 8, 512], BF16)
        EB, f_EB = sbuf_tmp(p, uniq + f"_EB{b}", [128, 8, 512], BF16)
        dr, f_dr = sbuf_tmp(p, uniq + f"_dr{b}", [1, 2, 5, 512], F32)
        rb_, f_rb = sbuf_tmp(p, uniq + f"_rb{b}", [128, 2, 5, 512], BF16)
        aoA, f_aoA = sbuf_tmp(p, uniq + f"_aoA{b}", [64, KC, 512], BF16)
        aoB, f_aoB = sbuf_tmp(p, uniq + f"_aoB{b}", [64, KC, 512], BF16)
        for s_ in range(NS):
            offs = [o for o in (-2, -1, 1, 2) if 0 <= s_ + o < NS]
            nv = len(offs)
            scol = s_ * 512
            for hp in range(8):
                rows = [(0, 64), (64, 128)]
                for kt in range(NS):
                    p.stage(uniq + f"_s{b}{s_}{hp}_{kt}")
                    for hi, (r0, r1) in enumerate(rows):
                        pbx = pb[hi]
                        p.comp("tensor", lambda t, kt=kt, r0=r0, r1=r1, pbx=pbx, hp=hp, qT=qT, kT=kT, scol=scol:
                               t.matmul(pbx[:, 0:512],
                                        kT[r0:r1, hp, scol + kt * 128:scol + (kt + 1) * 128],
                                        qT[r0:r1, hp, scol:scol + 512], start=True, stop=True))
                    p.stage(uniq + f"_e{b}{s_}{hp}_{kt}")
                    p.comp("scalar", lambda s, kt=kt, EA=EA: s.activation(
                        EA[:, kt, :], pb[0][:, 0:512], AF.Exp, scale=0.125))
                    p.comp("scalar", lambda s, kt=kt, EB=EB: s.activation(
                        EB[:, kt, :], pb[1][:, 0:512], AF.Exp, scale=0.125))
                for oi, off in enumerate(offs):
                    sn = s_ + off
                    p.stage(uniq + f"_sc{b}{s_}{hp}_{oi}")
                    for hi, (r0, r1) in enumerate(rows):
                        pbx = pb[hi]
                        p.comp("tensor", lambda t, sn=sn, r0=r0, r1=r1, pbx=pbx, hp=hp, qT=qT, kT=kT, scol=scol:
                               t.matmul(pbx[:, 0:512],
                                        kT[r0:r1, hp, sn * 512:(sn + 1) * 512:4],
                                        qT[r0:r1, hp, scol:scol + 512], start=True, stop=True))
                    p.stage(uniq + f"_ec{b}{s_}{hp}_{oi}")
                    p.comp("scalar", lambda s, oi=oi, EA=EA: s.activation(
                        EA[:, 4 + oi, :], pb[0][:, 0:512], AF.Exp, scale=0.125))
                    p.comp("scalar", lambda s, oi=oi, EB=EB: s.activation(
                        EB[:, 4 + oi, :], pb[1][:, 0:512], AF.Exp, scale=0.125))
                p.stage(uniq + f"_d{b}{s_}{hp}")
                def denloc(t, EA=EA, EB=EB):
                    for kt in range(NS):
                        t.matmul(pb[2][0:1, 0:512], c["ones_colb"][:], EA[:, kt, :],
                                 start=(kt == 0), stop=(kt == NS - 1))
                    for kt in range(NS):
                        t.matmul(pb[3][0:1, 0:512], c["ones_colb"][:], EB[:, kt, :],
                                 start=(kt == 0), stop=(kt == NS - 1))
                    last = None
                    for oi in range(nv):
                        t.matmul(pb[4][0:1, oi * 512:(oi + 1) * 512] if False else pb[4][0:1, 0:512],
                                 c["ones_colb"][:], EA[:, 4 + oi, :], start=True, stop=True)
                        last = t.matmul(pb[5][0:1, 0:512], c["ones_colb"][:], EB[:, 4 + oi, :],
                                        start=True, stop=True)
                    return last
                def denall(t, EA=EA, EB=EB, nv=nv):
                    last = None
                    for hi, E in ((0, EA), (1, EB)):
                        for kt in range(NS):
                            last = t.matmul(pb[2 + hi][0:1, 0:512], c["ones_colb"][:], E[:, kt, :],
                                            start=(kt == 0), stop=(kt == NS - 1))
                    return last
                p.comp("tensor", denall, n=2 * NS)
                p.stage(uniq + f"_dv{b}{s_}{hp}")
                p.comp("vector", lambda v: v.tensor_copy(dr[0:1, 0, 0, :], pb[2][0:1, 0:512]))
                p.comp("scalar", lambda s: s.copy(dr[0:1, 1, 0, :], pb[3][0:1, 0:512]))
                for oi in range(nv):
                    p.stage(uniq + f"_dc{b}{s_}{hp}_{oi}")
                    def dencr(t, oi=oi, EA=EA, EB=EB):
                        t.matmul(pb[2][0:1, 0:512], c["ones_colb"][:], EA[:, 4 + oi, :],
                                 start=True, stop=True)
                        return t.matmul(pb[3][0:1, 0:512], c["ones_colb"][:], EB[:, 4 + oi, :],
                                        start=True, stop=True)
                    p.comp("tensor", dencr, n=2)
                    p.stage(uniq + f"_dcv{b}{s_}{hp}_{oi}")
                    p.comp("vector", lambda v, oi=oi: v.tensor_copy(dr[0:1, 0, 1 + oi, :], pb[2][0:1, 0:512]))
                    p.comp("scalar", lambda s, oi=oi: s.copy(dr[0:1, 1, 1 + oi, :], pb[3][0:1, 0:512]))
                p.stage(uniq + f"_dr1{b}{s_}{hp}")
                p.comp("vector", lambda v, nv=nv: v.reciprocal(dr[0:1, :, 0:1 + nv, :], dr[0:1, :, 0:1 + nv, :]))
                if nv:
                    p.stage(uniq + f"_dr2{b}{s_}{hp}")
                    p.comp("vector", lambda v, nv=nv: v.tensor_single_scalar(
                        dr[0:1, :, 1:1 + nv, :], dr[0:1, :, 1:1 + nv, :], 0.25, ALU.mult))
                slots = [(hi, si) for hi in range(2) for si in range(1 + nv)]
                for g0 in range(0, len(slots), 4):
                    grp = slots[g0:g0 + 4]
                    p.stage(uniq + f"_db{b}{s_}{hp}_{g0}")
                    for gi, (hi, si) in enumerate(grp):
                        p.comp("tensor", lambda t, hi=hi, si=si, gi=gi: t.matmul(
                            pb[2 + gi][:, 0:512], c["ones_rowf"][0:1, 0:128], dr[0:1, hi, si, :],
                            start=True, stop=True))
                    p.stage(uniq + f"_dbe{b}{s_}{hp}_{g0}")
                    for gi, (hi, si) in enumerate(grp):
                        eng = "vector" if gi % 2 == 0 else "scalar"
                        if eng == "vector":
                            p.comp("vector", lambda v, hi=hi, si=si, gi=gi: v.tensor_copy(
                                rb_[:, hi, si, :], pb[2 + gi][:, 0:512]))
                        else:
                            p.comp("scalar", lambda s, hi=hi, si=si, gi=gi: s.copy(
                                rb_[:, hi, si, :], pb[2 + gi][:, 0:512]))
                p.stage(uniq + f"_en{b}{s_}{hp}")
                def enA(v, nv=nv, EA=EA):
                    last = None
                    for kt in range(NS):
                        last = v.tensor_tensor(EA[:, kt, :], EA[:, kt, :], rb_[:, 0, 0, :], ALU.mult)
                    for oi in range(nv):
                        last = v.tensor_tensor(EA[:, 4 + oi, :], EA[:, 4 + oi, :],
                                               rb_[:, 0, 1 + oi, :], ALU.mult)
                    return last
                p.comp("vector", enA, n=NS + nv)
                def enB(g, nv=nv, EB=EB):
                    last = None
                    for kt in range(NS):
                        last = g.tensor_tensor(EB[:, kt, :], EB[:, kt, :], rb_[:, 1, 0, :], ALU.mult)
                    for oi in range(nv):
                        last = g.tensor_tensor(EB[:, 4 + oi, :], EB[:, 4 + oi, :],
                                               rb_[:, 1, 1 + oi, :], ALU.mult)
                    return last
                p.comp("gpsimd", enB, n=NS + nv)
                p.stage(uniq + f"_o{b}{s_}{hp}")
                def omm(t, s_=s_, hp=hp, nv=nv, offs=offs, EA=EA, EB=EB, v_tok=v_tok, vd_tok=vd_tok):
                    last = None
                    for hi, E in ((0, EA), (1, EB)):
                        pbx = pb[hi]
                        hcol = hp * 128 + hi * 64
                        for kt in range(NS):
                            t.matmul(pbx[0:64, 0:512], v_tok[:, s_ * 4 + kt, hcol:hcol + 64],
                                     E[:, kt, :], start=(kt == 0), stop=False)
                        for oi, off in enumerate(offs):
                            last = t.matmul(pbx[0:64, 0:512], vd_tok[:, s_ + off, hcol:hcol + 64],
                                            E[:, 4 + oi, :], start=False, stop=(oi == nv - 1))
                    return last
                p.comp("tensor", omm, n=2 * (NS + nv))
                p.stage(uniq + f"_oe{b}{s_}{hp}")
                p.comp("vector", lambda v, hp=hp, aoA=aoA: v.tensor_copy(
                    aoA[:, hp, :], pb[0][0:64, 0:512]))
                p.comp("scalar", lambda s, hp=hp, aoB=aoB: s.copy(
                    aoB[:, hp, :], pb[1][0:64, 0:512]))
            p.stage(uniq + f"_aost{b}{s_}")
            p.dma("sync", lambda s, b=b, scol=scol, aoA=aoA: s.dma_start(
                out=aA_d.ap()[:, :, b * SEQ + scol:b * SEQ + scol + 512], in_=aoA[:]))
            p.dma("sync", lambda s, b=b, scol=scol, aoB=aoB: s.dma_start(
                out=aB_d.ap()[:, :, b * SEQ + scol:b * SEQ + scol + 512], in_=aoB[:]))
        f_aoB(); f_aoA(); f_rb(); f_dr(); f_EB(); f_EA()
        f_vdt(); f_vt(); f_kT(); f_qT()

    p.stage(uniq + "_wold")
    woA, f_woA = sbuf_tmp(p, uniq + "_woA", [64, KC, D], BF16)
    p.dma("gpsimd", lambda g: g.dma_start(out=woA[:], in_=w["WoA"].rearrange("(k p) o -> p k o", p=64)))
    woB, f_woB = sbuf_tmp(p, uniq + "_woB", [64, KC, D], BF16)
    p.dma("gpsimd", lambda g: g.dma_start(out=woB[:], in_=w["WoB"].rearrange("(k p) o -> p k o", p=64)))
    bob, f_bob = bias_bcast_row(p, c, uniq + "_bob", w["bo"], D, pb[0])
    aAi, f_aAi = sbuf_tmp(p, uniq + "_aAi", [64, KC, 512], BF16)
    aBi, f_aBi = sbuf_tmp(p, uniq + "_aBi", [64, KC, 512], BF16)
    yo, f_yo = sbuf_tmp(p, uniq + "_yo", [128, 4, D], F32)
    ydst = y_d.ap().rearrange("(n p) d -> p n d", p=128)
    for nb in range(NB):
        p.stage(uniq + f"_yld{nb}")
        p.dma("sync", lambda s, nb=nb: s.dma_start(
            out=aAi[:], in_=aA_d.ap()[:, :, nb * 512:(nb + 1) * 512]))
        p.dma("sync", lambda s, nb=nb: s.dma_start(
            out=aBi[:], in_=aB_d.ap()[:, :, nb * 512:(nb + 1) * 512]))
        for jt in range(4):
            p.stage(uniq + f"_ymm{nb}_{jt}")
            def ymm(t, jt=jt):
                last = None
                for half in range(2):
                    pbx = pb[4 + half]
                    for k in range(KC):
                        t.matmul(pbx[:, 0:512], aAi[:, k, jt * 128:(jt + 1) * 128],
                                 woA[:, k, half * 512:(half + 1) * 512],
                                 start=(k == 0), stop=False)
                    for k in range(KC):
                        last = t.matmul(pbx[:, 0:512], aBi[:, k, jt * 128:(jt + 1) * 128],
                                        woB[:, k, half * 512:(half + 1) * 512],
                                        start=False, stop=(k == KC - 1))
                return last
            p.comp("tensor", ymm, n=4 * KC)
            p.stage(uniq + f"_yev{nb}_{jt}")
            def yev(v, jt=jt):
                v.tensor_tensor(yo[:, jt, 0:512], pb[4][:, 0:512], bob[:, 0:512], ALU.add)
                return v.tensor_tensor(yo[:, jt, 512:1024], pb[5][:, 0:512], bob[:, 512:1024], ALU.add)
            p.comp("vector", yev, n=2)
        p.stage(uniq + f"_yst{nb}")
        p.dma("sync", lambda s, nb=nb: s.dma_start(out=ydst[:, nb * 4:(nb + 1) * 4], in_=yo[:]))
    f_yo(); f_aBi(); f_aAi(); f_bob(); f_woB(); f_woA()
    f_bv(); f_bk(); f_bq()

    epilogue_ln_add(p, c, y_d.ap(), xin_ap, xout_ap, uniq=uniq)


def _wo_split(wo):
    wo = np.asarray(wo, dtype=np.float32)
    rows = wo.reshape(16, 64, 1024)
    A = rows[0::2].reshape(512, 1024)
    B = rows[1::2].reshape(512, 1024)
    return A, B


WSPEC_L0A = [("Wq", "L0_attn_Wq", None), ("bq", "L0_attn_bq", None),
         ("Wk", "L0_attn_Wk", None), ("bk", "L0_attn_bk", None),
         ("Wv", "L0_attn_Wv", None), ("bv", "L0_attn_bv", None),
         ("WoA", "L0_attn_Wo", lambda a: _wo_split(a)[0]),
         ("WoB", "L0_attn_Wo", lambda a: _wo_split(a)[1]),
         ("bo", "L0_attn_bo", None)]


_doc_L0F = """L0_ffn: capacity-limited top-2 MoE (8 experts), SPARSE via indirect DMA.
Cross-core prefix counts via AllGather. CAPL=2048 local slots per (core, expert)."""
import numpy as np


from concourse.bass import IndirectOffsetOnAxis

E = 8
CAP = int(1.2 * (32768 // 8 + 1))
CAPL = 2048
HID = 4096


def build_l0_ffn(p, c, xin_ap, xout_ap, w, pb, pbt0, pbt1, uniq="l0f", rankmask_ap=None):
    p.gp_reg_requests["bcmoe"] = E * CAPL - 1
    y_d = p.dram(uniq + "_y", [T, D], F32)
    ln_d = p.dram(uniq + "_lntok", [T, D], BF16)
    rows_d = p.dram(uniq + "_rows", [4, T], F32)
    agin = p.dram(uniq + "_agin", [1, 16], F32)
    agout = p.dram(uniq + "_agout", [8, 16], F32, addr_space="Shared")
    lnG = p.dram(uniq + "_lnG", [E * CAPL, D], BF16)
    wbuf = p.dram(uniq + "_wbuf", [E * CAPL, 8], F32)
    zbuf = p.dram(uniq + "_zbuf", [E * CAPL, D], F32)

    lnT, f_lnT = sbuf_tmp(p, uniq + "_lnT", [128, KC, T], BF16)
    prologue_ln(p, c, xin_ap, lnT, pbt0, pbt1, ln_tok_dram=ln_d.ap(), uniq=uniq)

    p.stage(uniq + "_rwld")
    rw, f_rw = sbuf_tmp(p, uniq + "_rw", [128, KC, E], BF16, side="right")
    p.dma("gpsimd", lambda g: g.dma_start(out=rw[:], in_=w["router_w"].rearrange("(k p) o -> p k o", p=128),
                                          allow_slow_non_contiguous=True))
    rbb, f_rbb = sbuf_tmp(p, uniq + "_rbb", [128, E], F32, side="right")
    rbrow, f_rbr = sbuf_tmp(p, uniq + "_rbrow", [1, E], F32, side="right")
    p.dma("sync", lambda s: s.dma_start(out=rbrow[:], in_=w["router_b"].rearrange("(a d) -> a d", a=1)))
    rmask, f_rm = sbuf_tmp(p, uniq + "_rmask", [8, 1], F32, side="right")
    p.dma("sync", lambda s: s.dma_start(out=rmask[:], in_=rankmask_ap.rearrange("(a d) -> a d", d=1)))
    iota8c, f_i8c = sbuf_tmp(p, uniq + "_iota8c", [8, 1], F32, side="right")
    ones8c, f_o8c = sbuf_tmp(p, uniq + "_ones8c", [8, 1], F32, side="right")
    capcol, f_cc = sbuf_tmp(p, uniq + "_capcol", [8, 1], F32, side="right")
    p.comp("gpsimd", lambda g: g.iota(iota8c[:], [[1, 1]], channel_multiplier=1,
                                      allow_small_or_imprecise_dtypes=True))
    p.comp("gpsimd", lambda g: g.memset(ones8c[:], 1.0))
    p.stage(uniq + "_capc")
    p.comp("vector", lambda v: v.tensor_single_scalar(capcol[:], iota8c[:], float(CAPL), ALU.mult))
    p.stage(uniq + "_rbbc")
    p.comp("tensor", lambda t: t.matmul(pb[1][:, 0:E], c["ones_rowf"][0:1, 0:128], rbrow[:],
                                        start=True, stop=True))
    p.stage(uniq + "_rbbe")
    p.comp("vector", lambda v: v.tensor_copy(rbb[:], pb[1][:, 0:E]))

    tcols, f_tc = sbuf_tmp(p, uniq + "_tcols", [128, 4, NT], F32, side="right")
    sm8, f_sm8 = sbuf_tmp(p, uniq + "_sm8", [128, 8], F32, side="right")
    e8t, f_e8 = sbuf_tmp(p, uniq + "_e8t", [128, E], F32, side="right")
    srt, f_srt = sbuf_tmp(p, uniq + "_srt", [128, 8], F32, side="right")
    idxu, f_idxu = sbuf_tmp(p, uniq + "_idxu", [128, 8], U32, side="right")
    idxf, f_idxf = sbuf_tmp(p, uniq + "_idxf", [128, 8], F32, side="right")
    for ti in range(NT):
        p.stage(uniq + f"_rl{ti}")
        def rmm(t, ti=ti):
            last = None
            for k in range(KC):
                last = t.matmul(pb[0][:, 0:E], lnT[:, k, ti * 128:(ti + 1) * 128],
                                rw[:, k, :], start=(k == 0), stop=(k == KC - 1))
            return last
        p.comp("tensor", rmm, n=KC)
        p.stage(uniq + f"_rle{ti}")
        p.comp("vector", lambda v: v.tensor_tensor(e8t[:], pb[0][:, 0:E], rbb[:], ALU.add))
        p.stage(uniq + f"_rmx{ti}")
        p.comp("vector", lambda v: v.reduce_max(sm8[:, 0:1], e8t[:], axis=AX.X))
        p.stage(uniq + f"_rmx2{ti}")
        p.comp("vector", lambda v: v.tensor_single_scalar(sm8[:, 1:2], sm8[:, 0:1], -1.0, ALU.mult))
        p.stage(uniq + f"_rex{ti}")
        p.comp("scalar", lambda s: s.activation(e8t[:], e8t[:], AF.Exp, bias=sm8[:, 1:2],
                                                accum_out=sm8[:, 2:3]))
        p.stage(uniq + f"_rtp{ti}")
        p.comp("vector", lambda v: v.max(srt[:], e8t[:]))
        p.stage(uniq + f"_rti{ti}")
        p.comp("vector", lambda v: v.max_index(idxu[:], srt[:], e8t[:]))
        p.stage(uniq + f"_rif{ti}")
        p.comp("vector", lambda v: v.tensor_copy(idxf[:], idxu[:]))
        p.stage(uniq + f"_rw1{ti}")
        def rw1(v):
            v.tensor_tensor(sm8[:, 3:4], srt[:, 0:1], srt[:, 1:2], ALU.add)
            return v.tensor_scalar(sm8[:, 4:5], sm8[:, 2:3], 1e-9, None, ALU.mult)
        p.comp("vector", rw1, n=2)
        p.stage(uniq + f"_rw2{ti}")
        p.comp("vector", lambda v: v.tensor_tensor(sm8[:, 3:4], sm8[:, 3:4], sm8[:, 4:5], ALU.add))
        p.stage(uniq + f"_rw3{ti}")
        p.comp("vector", lambda v: v.reciprocal(sm8[:, 3:4], sm8[:, 3:4]))
        p.stage(uniq + f"_rw4{ti}")
        def rw4(v, ti=ti):
            v.tensor_scalar(tcols[:, 2, ti:ti + 1], srt[:, 0:1], sm8[:, 3:4], None, ALU.mult)
            return v.tensor_scalar(tcols[:, 3, ti:ti + 1], srt[:, 1:2], sm8[:, 3:4], None, ALU.mult)
        p.comp("vector", rw4, n=2)
        p.stage(uniq + f"_rv{ti}")
        def rv(v, ti=ti):
            v.tensor_scalar(sm8[:, 5:6], tcols[:, 2, ti:ti + 1], 1e-9, None, ALU.is_le)
            return v.tensor_scalar(sm8[:, 6:7], tcols[:, 3, ti:ti + 1], 1e-9, None, ALU.is_le)
        p.comp("vector", rv, n=2)
        p.stage(uniq + f"_ri{ti}")
        def ri(v, ti=ti):
            v.scalar_tensor_tensor(tcols[:, 0, ti:ti + 1], sm8[:, 5:6], 99.0, idxf[:, 0:1],
                                   ALU.mult, ALU.add)
            return v.scalar_tensor_tensor(tcols[:, 1, ti:ti + 1], sm8[:, 6:7], 99.0, idxf[:, 1:2],
                                          ALU.mult, ALU.add)
        p.comp("vector", ri, n=2)
    f_idxf(); f_idxu(); f_srt(); f_e8()

    trow, f_trow = sbuf_tmp(p, uniq + "_trow", [NT, 128], F32)
    for k in range(2):
        p.stage(uniq + f"_itr{k}")
        p.comp("tensor", lambda t, k=k: t.transpose(pb[2][0:NT, 0:128], tcols[:, k, :], c["idf"][:]))
        p.stage(uniq + f"_itre{k}")
        p.comp("vector", lambda v: v.tensor_copy(trow[:], pb[2][0:NT, 0:128]))
        p.stage(uniq + f"_itrd{k}")
        p.dma("sync", lambda s, k=k: s.dma_start(
            out=rows_d.ap()[k].rearrange("(n f) -> n f", n=NT), in_=trow[:]))
    ok_, f_ok = sbuf_tmp(p, uniq + "_ok", [8, 2, T], F32)
    kept, f_kept = sbuf_tmp(p, uniq + "_kept", [8, 2, T], F32)
    ones8T, f_o8T = sbuf_tmp(p, uniq + "_ones8T", [8, T], F32)
    posx, f_posx = sbuf_tmp(p, uniq + "_posx", [8, T], F32)
    ldrow, f_ldr = sbuf_tmp(p, uniq + "_ldrow", [1, T], F32)
    agsb, f_ag = sbuf_tmp(p, uniq + "_agsb", [8, 16], F32)
    agrow, f_agr = sbuf_tmp(p, uniq + "_agrow", [1, 16], F32)
    kcnt0, f_kc0 = sbuf_tmp(p, uniq + "_kcnt0", [8, 1], F32)
    offc, f_off = sbuf_tmp(p, uniq + "_offc", [8, 6], F32)
    p.stage(uniq + "_o8m")
    p.comp("gpsimd", lambda g: g.memset(ones8T[:], 1.0))
    for k in range(2):
        p.stage(uniq + f"_olr{k}")
        p.dma("sync", lambda s, k=k: s.dma_start(
            out=ldrow[:], in_=rows_d.ap()[k].rearrange("(a t) -> a t", a=1)))
        for nb in range(NB):
            p.stage(uniq + f"_obc{k}_{nb}")
            p.comp("tensor", lambda t, nb=nb: t.matmul(
                pb[0][0:8, 0:512], c["ones_rowf"][0:1, 0:8],
                ldrow[0:1, nb * 512:(nb + 1) * 512], start=True, stop=True))
            p.stage(uniq + f"_obe{k}_{nb}")
            p.comp("vector", lambda v, k=k, nb=nb: v.tensor_scalar(
                ok_[:, k, nb * 512:(nb + 1) * 512], pb[0][0:8, 0:512], iota8c[:, 0:1],
                None, ALU.is_equal))
        p.stage(uniq + f"_ocnt{k}")
        p.comp("vector", lambda v, k=k: v.reduce_sum(offc[:, k:k + 1], ok_[:, k, :], axis=AX.X))
        p.stage(uniq + f"_ocnt2{k}")
        p.comp("tensor", lambda t, k=k: t.transpose(pb[3][0:1, 0:8], offc[0:8, k:k + 1], c["idf"][0:8, 0:8]))
        p.stage(uniq + f"_ocnt3{k}")
        p.comp("vector", lambda v, k=k: v.tensor_copy(agrow[0:1, k * 8:(k + 1) * 8], pb[3][0:1, 0:8]))
    p.stage(uniq + "_agst")
    p.dma("sync", lambda s: s.dma_start(out=agin.ap(), in_=agrow[0:1, 0:16]))
    p.stage(uniq + "_ag")
    p.comp("gpsimd", lambda g: g.collective_compute(
        "AllGather", ALU.bypass, replica_groups=[[0, 1, 2, 3, 4, 5, 6, 7]],
        ins=[agin.ap()], outs=[agout.ap()]))
    p.stage(uniq + "_agld")
    p.dma("sync", lambda s: s.dma_start(out=agsb[:], in_=agout.ap()))
    p.stage(uniq + "_off1")
    def offmm(t):
        t.matmul(pb[0][0:8, 0:1], agsb[:, 0:8], rmask[:], start=True, stop=True)
        t.matmul(pb[1][0:8, 0:1], agsb[:, 0:8], ones8c[:], start=True, stop=True)
        return t.matmul(pb[2][0:8, 0:1], agsb[:, 8:16], rmask[:], start=True, stop=True)
    p.comp("tensor", offmm, n=3)
    p.stage(uniq + "_off2")
    def offev(v):
        v.tensor_copy(offc[:, 2:3], pb[0][0:8, 0:1])
        v.tensor_copy(offc[:, 3:4], pb[1][0:8, 0:1])
        return v.tensor_copy(offc[:, 4:5], pb[2][0:8, 0:1])
    p.comp("vector", offev, n=3)
    p.stage(uniq + "_off3")
    p.comp("vector", lambda v: v.tensor_tensor(offc[:, 5:6], offc[:, 4:5], offc[:, 3:4], ALU.add))
    for k, offsl in ((0, 2), (1, 5)):
        p.stage(uniq + f"_ps{k}")
        p.comp("vector", lambda v, k=k: v.tensor_tensor_scan(
            posx[:], ones8T[:], ok_[:, k, :], 0.0, ALU.mult, ALU.add))
        p.stage(uniq + f"_ps2{k}")
        p.comp("vector", lambda v, k=k: v.tensor_tensor(posx[:], posx[:], ok_[:, k, :], ALU.subtract))
        p.stage(uniq + f"_ps3{k}")
        p.comp("vector", lambda v, offsl=offsl: v.tensor_scalar(
            posx[:], posx[:], offc[:, offsl:offsl + 1], None, ALU.add))
        p.stage(uniq + f"_ps4{k}")
        p.comp("vector", lambda v: v.tensor_single_scalar(posx[:], posx[:], float(CAP), ALU.is_lt))
        p.stage(uniq + f"_ps5{k}")
        p.comp("vector", lambda v, k=k: v.tensor_tensor(kept[:, k, :], posx[:], ok_[:, k, :], ALU.mult))
        if k == 0:
            p.stage(uniq + "_kc0")
            p.comp("vector", lambda v: v.reduce_sum(kcnt0[:], kept[:, 0, :], axis=AX.X))
        if getattr(p, "dbg_moe", None) is not None:
            p.stage(uniq + f"_dbg{k}")
            p.dma("sync", lambda s, k=k: s.dma_start(out=p.dbg_moe.ap()[k], in_=kept[:, k, :]))
        p.stage(uniq + f"_lp{k}")
        p.comp("vector", lambda v, k=k: v.tensor_tensor_scan(
            posx[:], ones8T[:], kept[:, k, :], 0.0, ALU.mult, ALU.add))
        p.stage(uniq + f"_lp2{k}")
        p.comp("vector", lambda v, k=k: v.tensor_tensor(posx[:], posx[:], kept[:, k, :], ALU.subtract))
        p.stage(uniq + f"_lp3{k}")
        p.comp("vector", lambda v: v.tensor_scalar(posx[:], posx[:], capcol[:, 0:1], None, ALU.add))
        if k == 1:
            p.stage(uniq + "_lp3b")
            p.comp("vector", lambda v: v.tensor_scalar(posx[:], posx[:], kcnt0[:, 0:1], None, ALU.add))
        if getattr(p, "dbg_moe", None) is not None:
            p.stage(uniq + f"_dbg2{k}")
            p.dma("sync", lambda s, k=k: s.dma_start(out=p.dbg_moe.ap()[2 + k], in_=posx[:]))
        p.stage(uniq + f"_lp4{k}")
        p.comp("vector", lambda v, k=k: v.tensor_tensor(posx[:], posx[:], kept[:, k, :], ALU.mult))
        for nb in range(NB):
            p.stage(uniq + f"_sr{k}_{nb}")
            def srmm(t, k=k, nb=nb):
                t.matmul(pb[0][0:1, 0:512], ones8c[:], posx[:, nb * 512:(nb + 1) * 512],
                         start=True, stop=True)
                return t.matmul(pb[1][0:1, 0:512], ones8c[:], kept[:, k, nb * 512:(nb + 1) * 512],
                                start=True, stop=True)
            p.comp("tensor", srmm, n=2)
            p.stage(uniq + f"_sre{k}_{nb}")
            p.comp("vector", lambda v, nb=nb: v.tensor_scalar(
                ldrow[0:1, nb * 512:(nb + 1) * 512], pb[0][0:1, 0:512], 100000.0, None, ALU.add))
            p.stage(uniq + f"_sre2{k}_{nb}")
            p.comp("vector", lambda v, nb=nb: v.scalar_tensor_tensor(
                ldrow[0:1, nb * 512:(nb + 1) * 512], pb[1][0:1, 0:512], -100000.0,
                ldrow[0:1, nb * 512:(nb + 1) * 512], ALU.mult, ALU.add))
        p.stage(uniq + f"_srd{k}")
        p.dma("sync", lambda s, k=k: s.dma_start(
            out=rows_d.ap()[2 + k].rearrange("(a t) -> a t", a=1), in_=ldrow[:]))
    f_off(); f_kc0(); f_agr(); f_ag(); f_ldr(); f_posx(); f_o8T(); f_kept(); f_ok(); f_trow()
    f_lnT()

    slotf, f_sf = sbuf_tmp(p, uniq + "_slotf", [128, 2, NT], F32, side="right")
    sloti, f_si = sbuf_tmp(p, uniq + "_sloti", [128, 2, NT], I32, side="right")
    p.stage(uniq + "_slld")
    for k in range(2):
        p.dma("sync", lambda s, k=k: s.dma_start(
            out=slotf[:, k, :], in_=rows_d.ap()[2 + k].rearrange("(n pp) -> pp n", pp=128),
            allow_slow_non_contiguous=True))
    p.stage(uniq + "_slc")
    p.comp("vector", lambda v: v.tensor_copy(sloti[:], slotf[:]))

    zb, f_zb = sbuf_tmp(p, uniq + "_zb", [128, 16384], BF16)
    p.stage(uniq + "_zb0")
    p.comp("gpsimd", lambda g: g.memset(zb[:], 0.0))
    lnG_r = lnG.ap().rearrange("(n pp) d -> pp n d", pp=128)
    for i in range(8):
        p.stage(uniq + f"_zGd{i}")
        p.dma("sync", lambda s, i=i: s.dma_start(
            out=lnG_r[:, i * 16:(i + 1) * 16, :],
            in_=zb[:].rearrange("p (n d) -> p n d", d=D)))
    p.stage(uniq + "_zwb")
    p.dma("gpsimd", lambda g: g.dma_start(
        out=wbuf.ap().rearrange("(pp n) a -> pp (n a)", pp=128),
        in_=zb[:, 0:8 * (E * CAPL // 128)]))
    f_zb()
    lnld, f_lnld = sbuf_tmp(p, uniq + "_lnld", [128, D], BF16)
    wtmp, f_wtmp = sbuf_tmp(p, uniq + "_wtmp", [128, 2], F32)
    wtmp8, f_wtmp8 = sbuf_tmp(p, uniq + "_wtmp8", [128, 2, 8], F32)
    for ti in range(NT):
        p.stage(uniq + f"_scl{ti}")
        p.dma("sync", lambda s, ti=ti: s.dma_start(
            out=lnld[:], in_=ln_d.ap().rearrange("(n pp) d -> pp n d", pp=128)[:, ti]))
        p.comp("vector", lambda v, ti=ti: v.tensor_copy(
            wtmp8[:, 0, :], tcols[:, 2, ti:ti + 1].broadcast_to([128, 8])))
        p.comp("vector", lambda v, ti=ti: v.tensor_copy(
            wtmp8[:, 1, :], tcols[:, 3, ti:ti + 1].broadcast_to([128, 8])))
        p.stage(uniq + f"_sc{ti}")
        for k in range(2):
            p.dma("gpsimd", lambda g, ti=ti, k=k: g.indirect_dma_start(
                lnG.ap(), IndirectOffsetOnAxis(sloti[:, k, ti:ti + 1], 0), lnld[:], None,
                bounds_check=p.gp_regs["bcmoe"], oob_is_err=False))
            p.dma("gpsimd", lambda g, ti=ti, k=k: g.indirect_dma_start(
                wbuf.ap(), IndirectOffsetOnAxis(sloti[:, k, ti:ti + 1], 0),
                wtmp8[:, k, :], None,
                bounds_check=p.gp_regs["bcmoe"], oob_is_err=False))
    f_wtmp8(); f_wtmp(); f_lnld()

    NTS = CAPL // 128
    NBS = CAPL // 512
    for e in range(E):
        p.stage(uniq + f"_ew{e}")
        fc1e, f_fc1 = sbuf_tmp(p, uniq + f"_fc1{e}", [128, KC, HID], BF16)
        p.dma("gpsimd", lambda g, e=e: g.dma_start(
            out=fc1e[:], in_=w["fc1"][e * D:(e + 1) * D, :].rearrange("(k p) o -> p k o", p=128)))
        b1c, f_b1c = sbuf_tmp(p, uniq + f"_b1c{e}", [128, 32], F32)
        p.dma("sync", lambda s, e=e: s.dma_start(
            out=b1c[:], in_=w["fc1_b"][e].rearrange("(k pp) -> pp k", pp=128),
            allow_slow_non_contiguous=True))
        b2r, f_b2r = sbuf_tmp(p, uniq + f"_b2r{e}", [1, D], BF16)
        p.dma("gpsimd", lambda g, e=e: g.dma_start(
            out=b2r[:], in_=w["fc2_b"][e].rearrange("(a d) -> a d", a=1)))
        lsr, f_lsr = sbuf_tmp(p, uniq + f"_lsr{e}", [1, D], F32)
        p.dma("sync", lambda s, e=e: s.dma_start(
            out=lsr[:], in_=w["ls"][e].rearrange("(a d) -> a d", a=1)))
        lsb, f_lsb = sbuf_tmp(p, uniq + f"_lsb{e}", [128, D], F32)
        for hh in range(2):
            p.stage(uniq + f"_lsb{e}_{hh}")
            p.comp("tensor", lambda t, hh=hh: t.matmul(
                pb[0][:, 0:512], c["ones_rowf"][0:1, 0:128], lsr[0:1, hh * 512:(hh + 1) * 512],
                start=True, stop=True))
            p.stage(uniq + f"_lsbe{e}_{hh}")
            p.comp("vector", lambda v, hh=hh: v.tensor_copy(lsb[:, hh * 512:(hh + 1) * 512],
                                                            pb[0][:, 0:512]))
        wsc, f_wsc = sbuf_tmp(p, uniq + f"_wsc{e}", [128, NTS], F32)
        p.stage(uniq + f"_wscl{e}")
        p.dma("sync", lambda s, e=e: s.dma_start(
            out=wsc[:], in_=wbuf.ap()[e * CAPL:(e + 1) * CAPL, 0].rearrange("(n pp) -> pp n", pp=128),
            allow_slow_non_contiguous=True))
        xqT, f_xq = sbuf_tmp(p, uniq + f"_xq{e}", [128, KC, CAPL], BF16)
        p.stage(uniq + f"_xtr{e}")
        for ch in range(KC):
            p.dma("sync", lambda s, e=e, ch=ch: s.dma_start(
                out=xqT[:, ch, :], in_=lnG.ap()[e * CAPL:(e + 1) * CAPL, ch * 128:(ch + 1) * 128],
                transpose=True))
        hT, f_hT = sbuf_tmp(p, uniq + f"_hT{e}", [128, 32, 512], BF16)
        fc2h, f_fc2h = sbuf_tmp(p, uniq + f"_fc2h{e}", [128, 32, 512], BF16)
        ltile, f_lt = sbuf_tmp(p, uniq + f"_lt{e}", [128, D], BF16)
        zo, f_zo = sbuf_tmp(p, uniq + f"_zo{e}", [128, 512], F32)
        for nbs in range(NBS):
            proj_fm(p, fc1e, xqT, hT, HID, bias_cols=b1c, act=AF.Gelu, pbanks=pb[0:4],
                    name=uniq + f"_f1{e}_{nbs}", tblocks=[nbs], rhs_off=nbs * 512, out_off=nbs * 512)
            for hh in range(2):
                p.stage(uniq + f"_f2w{e}_{nbs}_{hh}")
                p.dma("gpsimd", lambda g, e=e, hh=hh: g.dma_start(
                    out=fc2h[:], in_=w["fc2"][e * HID:(e + 1) * HID, hh * 512:(hh + 1) * 512]
                    .rearrange("(k p) o -> p k o", p=128)))
                for jt in range(4):
                    sl0 = nbs * 512 + jt * 128
                    p.stage(uniq + f"_f2m{e}_{nbs}_{hh}_{jt}")
                    def f2mm(t, jt=jt, hh=hh, nbs=nbs):
                        for k in range(32):
                            t.matmul(pb[4][:, 0:512], hT[:, k, jt * 128:(jt + 1) * 128],
                                     fc2h[:, k, :], start=(k == 0), stop=False)
                        return t.matmul(pb[4][:, 0:512], c["ones_rowb"][0:1, 0:128],
                                        b2r[0:1, hh * 512:(hh + 1) * 512], start=False, stop=True)
                    p.comp("tensor", f2mm, n=33)
                    p.stage(uniq + f"_f2l{e}_{nbs}_{hh}_{jt}")
                    p.dma("sync", lambda s, e=e, sl0=sl0: s.dma_start(
                        out=ltile[:], in_=lnG.ap()[e * CAPL + sl0:e * CAPL + sl0 + 128, :]))
                    p.stage(uniq + f"_f2e{e}_{nbs}_{hh}_{jt}")
                    p.comp("vector", lambda v, hh=hh: v.tensor_tensor(
                        zo[:], pb[4][:, 0:512], lsb[:, hh * 512:(hh + 1) * 512], ALU.mult))
                    p.stage(uniq + f"_f2e2{e}_{nbs}_{hh}_{jt}")
                    p.comp("vector", lambda v, hh=hh: v.tensor_tensor(
                        zo[:], zo[:], ltile[:, hh * 512:(hh + 1) * 512], ALU.add))
                    p.stage(uniq + f"_f2s{e}_{nbs}_{hh}_{jt}")
                    p.comp("vector", lambda v, sl0=sl0: v.tensor_scalar(
                        zo[:], zo[:], wsc[:, sl0 // 128:sl0 // 128 + 1], None, ALU.mult))
                    p.stage(uniq + f"_f2d{e}_{nbs}_{hh}_{jt}")
                    p.dma("sync", lambda s, e=e, sl0=sl0, hh=hh: s.dma_start(
                        out=zbuf.ap()[e * CAPL + sl0:e * CAPL + sl0 + 128,
                                      hh * 512:(hh + 1) * 512], in_=zo[:]))
        f_zo(); f_lt(); f_fc2h(); f_hT(); f_xq(); f_wsc(); f_lsb(); f_lsr(); f_b2r(); f_b1c(); f_fc1()

    ytile, f_yt = sbuf_tmp(p, uniq + "_ytile", [128, D], F32)
    ydst = y_d.ap().rearrange("(n p) d -> p n d", p=128)
    for ti in range(NT):
        p.stage(uniq + f"_g0{ti}")
        p.comp("gpsimd", lambda g: g.memset(ytile[:], 0.0))
        p.stage(uniq + f"_g1{ti}")
        p.dma("gpsimd", lambda g, ti=ti: g.indirect_dma_start(
            ytile[:], None, zbuf.ap(), IndirectOffsetOnAxis(sloti[:, 0, ti:ti + 1], 0),
            bounds_check=p.gp_regs["bcmoe"], oob_is_err=False, compute_op=ALU.add))
        p.stage(uniq + f"_g2{ti}")
        p.dma("gpsimd", lambda g, ti=ti: g.indirect_dma_start(
            ytile[:], None, zbuf.ap(), IndirectOffsetOnAxis(sloti[:, 1, ti:ti + 1], 0),
            bounds_check=p.gp_regs["bcmoe"], oob_is_err=False, compute_op=ALU.add))
        p.stage(uniq + f"_g3{ti}")
        p.dma("sync", lambda s, ti=ti: s.dma_start(out=ydst[:, ti], in_=ytile[:]))
    f_yt(); f_si(); f_sf(); f_sm8(); f_tc()
    f_cc(); f_o8c(); f_i8c(); f_rm(); f_rbr()
    f_rbb(); f_rw()

    epilogue_ln_add(p, c, y_d.ap(), xin_ap, xout_ap, uniq=uniq)


def _stack_fc1(a):
    a = np.asarray(a, np.float32)
    return np.ascontiguousarray(np.concatenate([a[e].T for e in range(8)], axis=0))


def _stack_fc2(a):
    a = np.asarray(a, np.float32)
    return np.ascontiguousarray(np.concatenate([a[e].T for e in range(8)], axis=0))


WSPEC_L0F = [("router_w", "L0_ffn_router_w", "T"), ("router_b", "L0_ffn_router_b", None),
         ("fc1", "L0_ffn_fc1_w", _stack_fc1), ("fc1_b", "L0_ffn_fc1_b", None),
         ("fc2", "L0_ffn_fc2_w", _stack_fc2), ("fc2_b", "L0_ffn_fc2_b", None),
         ("ls", "L0_ffn_layer_scale", None)]


_doc_L1A = """L1_attn: gated linear attention (GLA), 8 heads x 128 dims, chunked scan C=128.
q = silu(x@WqT); f = sigmoid(x@WkT); g = log_sigmoid(f) = -ln(1+exp(-f)); k = 1-f;
v = x@WvT; per-head state S[d,e]; out = rmsnorm(o)*norm_w @ WoT."""
import numpy as np



SEQ = 2048
CH = 128
NCH = SEQ // CH
E44 = float(np.exp(44.0).astype(np.float32))


def build_l1_attn(p, c, xin_ap, xout_ap, w, pb, pbt0, pbt1, uniq="l1a"):
    y_d = p.dram(uniq + "_y", [T, D], F32)
    vtok_d = p.dram(uniq + "_vtok", [128, NT, D], BF16)
    ftok_d = p.dram(uniq + "_ftok", [128, NT, D], BF16)
    aout_d = p.dram(uniq + "_aout", [128, KC, T], BF16)
    rs_d = p.dram(uniq + "_rs", [1, T], F32)

    lnT, f_lnT = sbuf_tmp(p, uniq + "_lnT", [128, KC, T], BF16)
    prologue_ln(p, c, xin_ap, lnT, pbt0, pbt1, uniq=uniq)

    mask_tu, f_mtu = sbuf_tmp(p, uniq + "_mtu", [128, 128], BF16, side="right")
    lt2, f_lt2 = sbuf_tmp(p, uniq + "_lt2", [128, 128], F32, side="right")
    m44, f_m44 = sbuf_tmp(p, uniq + "_m44", [128, 1], F32, side="right")
    ones_ch, f_oc = sbuf_tmp(p, uniq + "_onch", [128, CH], F32, side="right")
    p.stage(uniq + "_mc")
    p.comp("gpsimd", lambda g: g.memset(m44[:], -44.0))
    p.comp("vector", lambda v: v.memset(ones_ch[:], 1.0))
    p.stage(uniq + "_mc2")
    p.comp("vector", lambda v: v.tensor_scalar(mask_tu[:], c["iota_row"][:], c["iota_col"][:, 0:1],
                                               None, ALU.is_ge))
    p.comp("gpsimd", lambda g: g.tensor_scalar(lt2[:], c["iota_row"][:], c["iota_col"][:, 0:1],
                                               None, ALU.is_lt))

    q_d = p.dram(uniq + "_q", [128, KC, T], BF16)
    fT, f_fT = sbuf_tmp(p, uniq + "_fT", [128, KC, T], BF16)
    pbuf, f_pb = sbuf_tmp(p, uniq + "_pb", [128, 1024], BF16)
    qbuf, f_qb = sbuf_tmp(p, uniq + "_qb", [128, KC, 512], BF16)
    for (wkey, act, dst_fm, dst_tok) in (("Wq", AF.Silu, "DRAM", None),
                                         ("Wk", AF.Sigmoid, fT, ftok_d),
                                         ("Wv", None, None, vtok_d)):
        p.stage(uniq + f"_w{wkey}")
        wsb, f_wsb = sbuf_tmp(p, uniq + f"_w{wkey}", [128, KC, D], BF16)
        p.dma("gpsimd", lambda g, wsb=wsb, wkey=wkey: g.dma_start(
            out=wsb[:], in_=w[wkey].rearrange("(k p) o -> p k o", p=128)))
        if dst_fm == "DRAM":
            for nb in range(NB):
                proj_fm(p, wsb, lnT, qbuf, D, act=act, pbanks=pb[0:4],
                        name=uniq + f"_pq{nb}", tblocks=[nb], out_off=nb * 512)
                p.stage(uniq + f"_qd{nb}")
                p.dma("sync", lambda s, nb=nb: s.dma_start(
                    out=q_d.ap()[:, :, nb * 512:(nb + 1) * 512], in_=qbuf[:]))
        elif dst_fm is not None:
            proj_fm(p, wsb, lnT, dst_fm, D, act=act, pbanks=pb[0:4], name=uniq + f"_p{wkey}")
        if dst_tok is not None:
            for ti in range(NT):
                p.stage(uniq + f"_t{wkey}{ti}")
                def tmm(t, ti=ti, wsb=wsb):
                    last = None
                    for half in range(2):
                        pbx = pb[4 + half]
                        for k in range(KC):
                            last = t.matmul(pbx[:, 0:512], lnT[:, k, ti * 128:(ti + 1) * 128],
                                            wsb[:, k, half * 512:(half + 1) * 512],
                                            start=(k == 0), stop=(k == KC - 1))
                    return last
                p.comp("tensor", tmm, n=2 * KC)
                p.stage(uniq + f"_te{wkey}{ti}")
                if act is not None:
                    p.comp("scalar", lambda s, act=act: s.activation(pbuf[:, 0:512], pb[4][:, 0:512], act))
                    p.comp("scalar", lambda s, act=act: s.activation(pbuf[:, 512:1024], pb[5][:, 0:512], act))
                else:
                    p.comp("vector", lambda v: v.tensor_copy(pbuf[:, 0:512], pb[4][:, 0:512]))
                    p.comp("vector", lambda v: v.tensor_copy(pbuf[:, 512:1024], pb[5][:, 0:512]))
                p.stage(uniq + f"_td{wkey}{ti}")
                p.dma("sync", lambda s, ti=ti, dst_tok=dst_tok: s.dma_start(
                    out=dst_tok.ap()[:, ti, :], in_=pbuf[:]))
        f_wsb()

    S_sb, f_S = sbuf_tmp(p, uniq + "_S", [128, 16, 128], BF16)
    G, f_G = sbuf_tmp(p, uniq + "_G", [128, KC, CH], F32)
    Gm, f_Gm = sbuf_tmp(p, uniq + "_Gmx", [128, KC, CH], F32)
    eG, f_eG = sbuf_tmp(p, uniq + "_eG", [128, KC, CH], F32)
    qmid, f_qm = sbuf_tmp(p, uniq + "_qmid", [128, KC, CH], BF16)
    qful, f_qf = sbuf_tmp(p, uniq + "_qful", [128, KC, CH], BF16)
    kbar, f_kb = sbuf_tmp(p, uniq + "_kbar", [128, KC, CH], BF16)
    khat, f_kh = sbuf_tmp(p, uniq + "_khat", [128, KC, CH], BF16)
    gsl, f_gs = sbuf_tmp(p, uniq + "_gsl", [128, KC, CH], F32)
    gtok, f_gt = sbuf_tmp(p, uniq + "_gtok", [128, D], F32)
    qsl, f_qsl = sbuf_tmp(p, uniq + "_qsl", [128, KC, CH], BF16)
    vt, f_vt = sbuf_tmp(p, uniq + "_vt", [128, D], BF16)
    ft2, f_ft2 = sbuf_tmp(p, uniq + "_ft2", [128, D], BF16)
    Asb, f_As = sbuf_tmp(p, uniq + "_Asb", [128, 8, 128], BF16)
    aoch, f_ao = sbuf_tmp(p, uniq + "_aoch", [128, KC, CH], BF16)
    cols, f_cols = sbuf_tmp(p, uniq + "_cols", [128, 8, 4], F32)
    p.stage(uniq + "_S0")
    p.comp("gpsimd", lambda g: g.memset(S_sb[:], 0.0))
    for b in range(2):
        for ci in range(NCH):
            ti = b * NCH + ci
            cc = ti * CH
            u = uniq + f"_{b}_{ci}"
            p.stage(u + "ld")
            p.dma("sync", lambda s, ti=ti: s.dma_start(out=vt[:], in_=vtok_d.ap()[:, ti, :]))
            p.dma("sync", lambda s, ti=ti: s.dma_start(out=ft2[:], in_=ftok_d.ap()[:, ti, :]))
            p.dma("sync", lambda s, cc=cc: s.dma_start(out=qsl[:], in_=q_d.ap()[:, :, cc:cc + CH]))
            p.stage(u + "g1")
            p.comp("scalar", lambda s, cc=cc: s.activation(gsl[:], fT[:, :, cc:cc + CH], AF.Exp, scale=-1.0))
            p.stage(u + "g2")
            p.comp("scalar", lambda s: s.activation(gsl[:], gsl[:], AF.Ln, bias=c["ones_colf"][:, 0:1]))
            p.stage(u + "g3")
            p.comp("vector", lambda v: v.tensor_scalar(gsl[:], gsl[:], -1.0, None, ALU.mult))
            p.stage(u + "sc")
            def scans(v):
                last = None
                for h in range(KC):
                    last = v.tensor_tensor_scan(G[:, h, :], ones_ch[:], gsl[:, h, :],
                                                0.0, ALU.mult, ALU.add)
                return last
            p.comp("vector", scans, n=KC)
            p.stage(u + "cols")
            def colcp(v):
                last = None
                for h in range(KC):
                    v.tensor_copy(cols[:, h, 0:1], G[:, h, 63:64])
                    last = v.tensor_copy(cols[:, h, 1:2], G[:, h, 127:128])
                return last
            p.comp("vector", colcp, n=2 * KC)
            p.stage(u + "ecols")
            p.comp("scalar", lambda s: s.activation(cols[:, :, 2:3], cols[:, :, 0:1], AF.Exp))
            p.comp("scalar", lambda s: s.activation(cols[:, :, 3:4], cols[:, :, 1:2], AF.Exp))
            p.stage(u + "gm")
            def gmx(v):
                last = None
                for h in range(KC):
                    last = v.tensor_scalar(Gm[:, h, :], G[:, h, :], cols[:, h, 0:1], None, ALU.subtract)
                return last
            p.comp("vector", gmx, n=KC)
            p.stage(u + "e1")
            p.comp("scalar", lambda s: s.activation(eG[:], Gm[:], AF.Exp))
            p.stage(u + "m1")
            p.comp("vector", lambda v: v.tensor_tensor(qmid[:], qsl[:], eG[:], ALU.mult))
            p.stage(u + "e2")
            p.comp("scalar", lambda s: s.activation(eG[:], Gm[:], AF.Exp, scale=-1.0))
            p.stage(u + "m2k")
            p.comp("vector", lambda v, cc=cc: v.tensor_scalar(Gm[:], fT[:, :, cc:cc + CH],
                                                              -1.0, 1.0, ALU.mult, ALU.add))
            p.stage(u + "m2")
            p.comp("vector", lambda v: v.tensor_tensor(kbar[:], Gm[:], eG[:], ALU.mult))
            p.stage(u + "m3")
            def qfx(v):
                last = None
                for h in range(KC):
                    last = v.tensor_scalar(qful[:, h, :], qmid[:, h, :], cols[:, h, 2:3], None, ALU.mult)
                return last
            p.comp("vector", qfx, n=KC)
            p.stage(u + "tg1")
            p.comp("scalar", lambda s: s.activation(gtok[:], ft2[:], AF.Exp, scale=-1.0))
            p.stage(u + "tg2")
            p.comp("scalar", lambda s: s.activation(gtok[:], gtok[:], AF.Ln, bias=c["ones_colf"][:, 0:1]))
            p.stage(u + "tg3")
            p.comp("vector", lambda v: v.tensor_scalar(gtok[:], gtok[:], -1.0, None, ALU.mult))
            p.stage(u + "tmm")
            def trimm(t):
                last = None
                for h in range(KC):
                    pbx = pb[h % 2]
                    last = t.matmul(pbx[:, (h // 2) * 128:(h // 2) * 128 + 128],
                                    lt2[:], gtok[:, h * 128:(h + 1) * 128],
                                    start=True, stop=True)
                return last
            p.comp("tensor", trimm, n=KC)
            p.stage(u + "tev")
            def tevx(s):
                last = None
                for h in range(KC):
                    pbx = pb[h % 2]
                    last = s.activation(eG[:, h, :], pbx[:, (h // 2) * 128:(h // 2) * 128 + 128],
                                        AF.Exp, bias=m44[:, 0:1])
                return last
            p.comp("scalar", tevx, n=KC)
            p.stage(u + "kh1")
            p.comp("vector", lambda v: v.tensor_scalar(gtok[:], ft2[:], -1.0, 1.0, ALU.mult, ALU.add))
            p.stage(u + "kh2")
            p.comp("vector", lambda v: v.tensor_tensor(
                khat[:].rearrange("p a b -> p (a b)"), gtok[:], eG[:].rearrange("p a b -> p (a b)"), ALU.mult))
            p.stage(u + "amm")
            def amm(t):
                last = None
                for h in range(KC):
                    pbx = pb[2 + h % 2]
                    last = t.matmul(pbx[:, (h // 2) * 128:(h // 2) * 128 + 128],
                                    kbar[:, h, :], qmid[:, h, :], start=True, stop=True)
                return last
            p.comp("tensor", amm, n=KC)
            p.stage(u + "aev")
            def aev(v):
                last = None
                for h in range(KC):
                    pbx = pb[2 + h % 2]
                    last = v.tensor_tensor(Asb[:, h, :], pbx[:, (h // 2) * 128:(h // 2) * 128 + 128],
                                           mask_tu[:], ALU.mult)
                return last
            p.comp("vector", aev, n=KC)
            p.stage(u + "omm")
            def omm(t, b=b):
                last = None
                for h in range(KC):
                    pbx = pb[4 + h % 2]
                    sl = pbx[:, (h // 2) * 128:(h // 2) * 128 + 128]
                    t.matmul(sl, S_sb[:, b * 8 + h, :], qful[:, h, :], start=True, stop=False)
                    last = t.matmul(sl, vt[:, h * 128:(h + 1) * 128], Asb[:, h, :],
                                    start=False, stop=True)
                return last
            p.comp("tensor", omm, n=2 * KC)
            p.stage(u + "oev")
            def oev(v):
                last = None
                for h in range(KC):
                    pbx = pb[4 + h % 2]
                    last = v.tensor_copy(aoch[:, h, :], pbx[:, (h // 2) * 128:(h // 2) * 128 + 128])
                return last
            p.comp("vector", oev, n=KC)
            p.stage(u + "odma")
            p.dma("sync", lambda s, cc=cc: s.dma_start(out=aout_d.ap()[:, :, cc:cc + CH], in_=aoch[:]))
            p.stage(u + "smm")
            def smm(t):
                last = None
                for h in range(KC):
                    pbx = pb[0 + h % 2]
                    last = t.matmul(pbx[:, (h // 2) * 128:(h // 2) * 128 + 128],
                                    khat[:, h, :], vt[:, h * 128:(h + 1) * 128],
                                    start=True, stop=True)
                return last
            p.comp("tensor", smm, n=KC)
            p.stage(u + "sdec")
            def sdec(v, b=b):
                last = None
                for h in range(KC):
                    last = v.tensor_scalar(S_sb[:, b * 8 + h, :], S_sb[:, b * 8 + h, :],
                                           cols[:, h, 3:4], None, ALU.mult)
                return last
            p.comp("vector", sdec, n=KC)
            p.stage(u + "supd")
            def supd(v, b=b):
                last = None
                for h in range(KC):
                    pbx = pb[0 + h % 2]
                    last = v.scalar_tensor_tensor(S_sb[:, b * 8 + h, :],
                                                  pbx[:, (h // 2) * 128:(h // 2) * 128 + 128],
                                                  E44, S_sb[:, b * 8 + h, :], ALU.mult, ALU.add)
                return last
            p.comp("vector", supd, n=KC)
    for ff in (f_cols, f_ao, f_As, f_ft2, f_vt, f_qsl, f_gt, f_gs, f_kh, f_kb,
               f_qf, f_qm, f_eG, f_Gm, f_G, f_S):
        ff()
    f_qb(); f_pb(); f_fT(); f_lnT()

    sq, f_sq = sbuf_tmp(p, uniq + "_sq", [128, KC, 512], F32)
    rsrow, f_rs = sbuf_tmp(p, uniq + "_rsr", [1, T], F32)
    ain, f_ain = sbuf_tmp(p, uniq + "_ain", [128, KC, 512], BF16)
    for nb in range(NB):
        p.stage(uniq + f"_ra{nb}")
        p.dma("sync", lambda s, nb=nb: s.dma_start(
            out=ain[:], in_=aout_d.ap()[:, :, nb * 512:(nb + 1) * 512]))
        p.stage(uniq + f"_rb{nb}")
        def rsq(s):
            last = None
            for k in range(KC):
                last = s.activation(sq[:, k, :], ain[:, k, :], AF.Square)
            return last
        p.comp("scalar", rsq, n=KC)
        p.stage(uniq + f"_rc{nb}")
        def rmm(t):
            last = None
            for k in range(KC):
                last = t.matmul(pb[0][0:1, 0:512], c["ones_colf"][:], sq[:, k, :],
                                start=(k == 0), stop=(k == KC - 1))
            return last
        p.comp("tensor", rmm, n=KC)
        p.stage(uniq + f"_rd{nb}")
        p.comp("vector", lambda v, nb=nb: v.tensor_scalar(
            rsrow[0:1, nb * 512:(nb + 1) * 512], pb[0][0:1, 0:512], 1.0 / D, 1e-6, ALU.mult, ALU.add))
    p.stage(uniq + "_re")
    p.comp("scalar", lambda s: s.activation(rsrow[:], rsrow[:], AF.Sqrt))
    p.stage(uniq + "_rf")
    p.comp("vector", lambda v: v.reciprocal(rsrow[:], rsrow[:]))
    p.stage(uniq + "_rg")
    p.dma("sync", lambda s: s.dma_start(out=rs_d.ap(), in_=rsrow[:]))
    f_ain(); f_rs(); f_sq()

    rscol, f_rsc = sbuf_tmp(p, uniq + "_rsc", [128, NT], F32, side="right")
    p.stage(uniq + "_rh")
    p.dma("sync", lambda s: s.dma_start(
        out=rscol[:], in_=rs_d.ap()[0].rearrange("(n pp) -> pp n", pp=128),
        allow_slow_non_contiguous=True))
    p.stage(uniq + "_wo")
    wo, f_wo = sbuf_tmp(p, uniq + "_wo", [128, KC, D], BF16)
    p.dma("gpsimd", lambda g: g.dma_start(out=wo[:], in_=w["Wo"].rearrange("(k p) o -> p k o", p=128)))
    nwc, f_nw = sbuf_tmp(p, uniq + "_nwc", [128, KC], F32, side="right")
    p.dma("sync", lambda s: s.dma_start(out=nwc[:], in_=w["norm_w"].rearrange("(k pp) -> pp k", pp=128),
                                        allow_slow_non_contiguous=True))
    p.stage(uniq + "_wof")
    def wofold(v):
        last = None
        for k in range(KC):
            last = v.tensor_scalar(wo[:, k, :], wo[:, k, :], nwc[:, k:k + 1], None, ALU.mult)
        return last
    p.comp("vector", wofold, n=KC)
    ain2, f_ain2 = sbuf_tmp(p, uniq + "_ain2", [128, KC, 512], BF16)
    yo, f_yo = sbuf_tmp(p, uniq + "_yo", [128, 4, D], F32)
    ydst = y_d.ap().rearrange("(n p) d -> p n d", p=128)
    for nb in range(NB):
        p.stage(uniq + f"_ya{nb}")
        p.dma("sync", lambda s, nb=nb: s.dma_start(
            out=ain2[:], in_=aout_d.ap()[:, :, nb * 512:(nb + 1) * 512]))
        for jt in range(4):
            ti = nb * 4 + jt
            p.stage(uniq + f"_ym{nb}_{jt}")
            def ymm(t, jt=jt):
                last = None
                for half in range(2):
                    pbx = pb[4 + half]
                    for k in range(KC):
                        last = t.matmul(pbx[:, 0:512], ain2[:, k, jt * 128:(jt + 1) * 128],
                                        wo[:, k, half * 512:(half + 1) * 512],
                                        start=(k == 0), stop=(k == KC - 1))
                return last
            p.comp("tensor", ymm, n=2 * KC)
            p.stage(uniq + f"_ye{nb}_{jt}")
            def yev(v, jt=jt, ti=ti):
                v.tensor_scalar(yo[:, jt, 0:512], pb[4][:, 0:512], rscol[:, ti:ti + 1], None, ALU.mult)
                return v.tensor_scalar(yo[:, jt, 512:1024], pb[5][:, 0:512], rscol[:, ti:ti + 1],
                                       None, ALU.mult)
            p.comp("vector", yev, n=2)
        p.stage(uniq + f"_yd{nb}")
        p.dma("sync", lambda s, nb=nb: s.dma_start(out=ydst[:, nb * 4:(nb + 1) * 4], in_=yo[:]))
    f_yo(); f_ain2(); f_wo()
    f_nw(); f_rsc()
    f_oc(); f_m44(); f_lt2(); f_mtu()

    epilogue_ln_add(p, c, y_d.ap(), xin_ap, xout_ap, uniq=uniq)


WSPEC_L1A = [("Wq", "L1_attn_Wq", "T"), ("Wk", "L1_attn_Wk", "T"),
         ("Wv", "L1_attn_Wv", "T"), ("Wo", "L1_attn_Wo", "T"),
         ("norm_w", "L1_attn_norm_w", None)]


_doc_L1F = """L1_ffn: h = bitlinear(ln, w_fc) + gate*(ln @ vT @ fbT @ uT); out = bitlinear(relu(h)^2, w_proj)."""
import numpy as np




def prep_ternary(p, c, w_ap, din, dout, weff, pb, uniq, dblk=512):
    KN = din // 128
    ND = dout // dblk
    wf32, f_w = sbuf_tmp(p, uniq + "_pw", [128, KN, dblk], F32)
    aw, f_aw = sbuf_tmp(p, uniq + "_paw", [128, dblk], F32)
    t1, f_t1 = sbuf_tmp(p, uniq + "_pt1", [128, KN, dblk], BF16)
    t2, f_t2 = sbuf_tmp(p, uniq + "_pt2", [128, KN, dblk], BF16)
    srow, f_sr = sbuf_tmp(p, uniq + "_psr", [1, dblk], F32)
    sb_, f_sb = sbuf_tmp(p, uniq + "_psb", [128, dblk], F32)
    rsb, f_rsb = sbuf_tmp(p, uniq + "_prsb", [128, dblk], F32)
    for db in range(ND):
        p.stage(uniq + f"_pld{db}")
        p.dma("sync", lambda s, db=db: s.dma_start(
            out=wf32[:], in_=w_ap[:, db * dblk:(db + 1) * dblk].rearrange("(k p) o -> p k o", p=128)))
        for k in range(KN):
            p.stage(uniq + f"_pab{db}_{k}")
            p.comp("scalar", lambda s, k=k: s.activation(aw[:], wf32[:, k, :], AF.Abs))
            p.stage(uniq + f"_pmm{db}_{k}")
            p.comp("tensor", lambda t, k=k: t.matmul(
                pb[0][0:1, 0:dblk], c["ones_colf"][:], aw[:],
                start=(k == 0), stop=(k == KN - 1)))
        p.stage(uniq + f"_psc{db}")
        p.comp("vector", lambda v: v.tensor_scalar(
            srow[:], pb[0][0:1, 0:dblk], 1.0 / din, 1e-5, ALU.mult, ALU.max))
        p.stage(uniq + f"_psbmm{db}")
        p.comp("tensor", lambda t: t.matmul(pb[1][:, 0:dblk], c["ones_rowf"][0:1, 0:128],
                                            srow[:], start=True, stop=True))
        p.stage(uniq + f"_psbe{db}")
        p.comp("vector", lambda v: v.tensor_copy(sb_[:], pb[1][:, 0:dblk]))
        p.stage(uniq + f"_prs{db}")
        p.comp("vector", lambda v: v.reciprocal(rsb[:], sb_[:]))
        p.stage(uniq + f"_pa{db}")
        def pa(v):
            last = None
            for k in range(KN):
                last = v.tensor_tensor(wf32[:, k, :], wf32[:, k, :], rsb[:], ALU.mult)
            return last
        p.comp("vector", pa, n=KN)
        p.stage(uniq + f"_pth{db}")
        def pth(v):
            last = None
            for k in range(KN):
                last = v.tensor_single_scalar(t1[:, k, :], wf32[:, k, :], 0.5, ALU.is_ge)
            return last
        p.comp("vector", pth, n=KN)
        def pth2(g):
            last = None
            for k in range(KN):
                last = g.tensor_single_scalar(t2[:, k, :], wf32[:, k, :], -0.5, ALU.is_le)
            return last
        p.comp("gpsimd", pth2, n=KN)
        p.stage(uniq + f"_ptd{db}")
        def ptd(v):
            last = None
            for k in range(KN):
                last = v.tensor_tensor(t1[:, k, :], t1[:, k, :], t2[:, k, :], ALU.subtract)
            return last
        p.comp("vector", ptd, n=KN)
        p.stage(uniq + f"_pwe{db}")
        def pwe(v, db=db):
            last = None
            for k in range(KN):
                last = v.tensor_tensor(weff[:, k, db * dblk:(db + 1) * dblk],
                                       t1[:, k, :], sb_[:], ALU.mult)
            return last
        p.comp("vector", pwe, n=KN)
    f_rsb(); f_sb(); f_sr(); f_t2(); f_t1(); f_aw(); f_w()


def build_l1_ffn(p, c, xin_ap, xout_ap, w, pb, pbt0, pbt1, uniq="l1f"):
    HID = 4096
    y_d = p.dram(uniq + "_y", [T, D], F32)
    h_d = p.dram(uniq + "_h", [128, 32, T], BF16)

    lnT, f_lnT = sbuf_tmp(p, uniq + "_lnT", [128, KC, T], BF16)
    prologue_ln(p, c, xin_ap, lnT, pbt0, pbt1, uniq=uniq)

    weff1, f_we1 = sbuf_tmp(p, uniq + "_weff1", [128, KC, HID], BF16)
    prep_ternary(p, c, w["w_fc"], D, HID, weff1, pb, uniq + "1")

    p.stage(uniq + "_smld")
    vT, f_vT = sbuf_tmp(p, uniq + "_vT", [128, KC, 16], BF16)
    p.dma("gpsimd", lambda g: g.dma_start(out=vT[:], in_=w["v"].rearrange("(k p) o -> p k o", p=128),
                                          allow_slow_non_contiguous=True))
    fbT, f_fbT = sbuf_tmp(p, uniq + "_fbT", [16, 16], BF16)
    p.dma("gpsimd", lambda g: g.dma_start(out=fbT[:], in_=w["fast_b"]))
    uT, f_uT = sbuf_tmp(p, uniq + "_uT", [128, HID], BF16)
    p.comp("gpsimd", lambda g: g.memset(uT[:], 0.0))
    p.stage(uniq + "_uld")
    p.dma("gpsimd", lambda g: g.dma_start(out=uT[0:16, :], in_=w["u"]))
    gsc, f_gsc = sbuf_tmp(p, uniq + "_g", [1, 1], F32, side="right")
    p.dma("sync", lambda s: s.dma_start(out=gsc[:], in_=w["gate"].rearrange("(a d) -> a d", a=1)))
    gcol, f_gcol = sbuf_tmp(p, uniq + "_gcv", [16, 1], F32, side="right")

    z1T, f_z1 = sbuf_tmp(p, uniq + "_z1", [16, T], BF16)
    z2T, f_z2 = sbuf_tmp(p, uniq + "_z2", [128, T], BF16)
    p.stage(uniq + "_z2zero")
    p.comp("gpsimd", lambda g: g.memset(z2T[:], 0.0))
    p.stage(uniq + "_gcolmm")
    p.comp("tensor", lambda t: t.matmul(pb[1][0:16, 0:1], c["ones_rowf"][0:1, 0:16],
                                        gsc[:], start=True, stop=True))
    p.stage(uniq + "_gcolev")
    p.comp("vector", lambda v: v.tensor_copy(gcol[:], pb[1][0:16, 0:1]))
    for nb in range(NB):
        p.stage(uniq + f"_z1_{nb}")
        def z1mm(t, nb=nb):
            last = None
            for k in range(KC):
                last = t.matmul(pb[0][0:16, 0:512], vT[:, k, :],
                                lnT[:, k, nb * 512:(nb + 1) * 512],
                                start=(k == 0), stop=(k == KC - 1))
            return last
        p.comp("tensor", z1mm, n=KC)
        p.stage(uniq + f"_z1e_{nb}")
        p.comp("vector", lambda v, nb=nb: v.tensor_copy(z1T[:, nb * 512:(nb + 1) * 512],
                                                        pb[0][0:16, 0:512]))
        p.stage(uniq + f"_z2_{nb}")
        p.comp("tensor", lambda t, nb=nb: t.matmul(pb[1][0:16, 0:512], fbT[:],
                                                   z1T[:, nb * 512:(nb + 1) * 512],
                                                   start=True, stop=True))
        p.stage(uniq + f"_z2e_{nb}")
        p.comp("scalar", lambda s, nb=nb: s.activation(z2T[0:16, nb * 512:(nb + 1) * 512],
                                                       pb[1][0:16, 0:512], AF.Identity,
                                                       scale=gcol[:]))

    hb = [sbuf_tmp(p, uniq + f"_hb{j}", [128, 512], BF16) for j in range(2)]
    hsq = [sbuf_tmp(p, uniq + f"_hsq{j}", [128, 512], BF16) for j in range(2)]
    pairs = [(m, nb) for m in range(32) for nb in range(NB)]
    NPAIR = len(pairs)
    for i in range(NPAIR + 3):
        p.stage(uniq + f"_h{i}")
        if i < NPAIR:
            m, nb = pairs[i]
            pbx = pb[i % 2]
            def hmm(t, m=m, nb=nb, pbx=pbx):
                for k in range(KC):
                    t.matmul(pbx[:, 0:512], weff1[:, k, m * 128:(m + 1) * 128],
                             lnT[:, k, nb * 512:(nb + 1) * 512],
                             start=(k == 0), stop=False)
                return t.matmul(pbx[:, 0:512], uT[:, m * 128:(m + 1) * 128],
                                z2T[:, nb * 512:(nb + 1) * 512], start=False, stop=True)
            p.comp("tensor", hmm, n=KC + 1)
        if 1 <= i <= NPAIR:
            j = i - 1
            p.comp("scalar", lambda s, j=j: s.activation(hb[j % 2][0][:], pb[j % 2][:, 0:512], AF.Relu))
        if 2 <= i <= NPAIR + 1:
            j = i - 2
            p.comp("vector", lambda v, j=j: v.tensor_tensor(
                hsq[j % 2][0][:], hb[j % 2][0][:], hb[j % 2][0][:], ALU.mult))
        if 3 <= i <= NPAIR + 2:
            j = i - 3
            m, nb = pairs[j]
            p.dma("sync", lambda s, j=j, m=m, nb=nb: s.dma_start(
                out=h_d.ap()[:, m, nb * 512:(nb + 1) * 512], in_=hsq[j % 2][0][:]))
    hsq[1][1](); hsq[0][1](); hb[1][1](); hb[0][1]()
    p.sb_weff1 = weff1
    p.sb_z2T = z2T
    if not getattr(p, "dbg_keep", False):
        f_z2(); f_z1(); f_uT(); f_fbT(); f_vT()
        f_we1()
    if not getattr(p, "dbg_keep", False):
        f_lnT()

    if getattr(p, "dbg_keep", False):
        f_gcol(); f_gsc()
        return
    weff2, f_we2 = sbuf_tmp(p, uniq + "_weff2", [128, 32, D], BF16)
    prep_ternary(p, c, w["w_proj"], HID, D, weff2, pb, uniq + "2", dblk=256)
    hA, f_hA = sbuf_tmp(p, uniq + "_hA", [128, 32, 512], BF16)
    hB, f_hB = sbuf_tmp(p, uniq + "_hB", [128, 32, 512], BF16)
    yo, f_yo = sbuf_tmp(p, uniq + "_yo", [128, 4, D], F32)
    ydst = y_d.ap().rearrange("(n p) d -> p n d", p=128)
    p.stage(uniq + "_yld0")
    p.dma("sync", lambda s: s.dma_start(out=hA[:], in_=h_d.ap()[:, :, 0:512]))
    for nb in range(NB):
        hsrc = hA if nb % 2 == 0 else hB
        p.stage(uniq + f"_ypre{nb}")
        if nb + 1 < NB:
            nxt = hB if nb % 2 == 0 else hA
            p.dma("sync", lambda s, nb=nb, nxt=nxt: s.dma_start(
                out=nxt[:], in_=h_d.ap()[:, :, (nb + 1) * 512:(nb + 2) * 512]))
        for jt in range(4):
            p.stage(uniq + f"_ymm{nb}_{jt}")
            def ymm(t, jt=jt, hsrc=hsrc):
                last = None
                for half in range(2):
                    pbx = pb[4 + half]
                    for k in range(32):
                        last = t.matmul(pbx[:, 0:512], hsrc[:, k, jt * 128:(jt + 1) * 128],
                                        weff2[:, k, half * 512:(half + 1) * 512],
                                        start=(k == 0), stop=(k == 31))
                return last
            p.comp("tensor", ymm, n=64)
            p.stage(uniq + f"_yev{nb}_{jt}")
            def yev(v, jt=jt):
                v.tensor_copy(yo[:, jt, 0:512], pb[4][:, 0:512])
                return v.tensor_copy(yo[:, jt, 512:1024], pb[5][:, 0:512])
            p.comp("vector", yev, n=2)
        p.stage(uniq + f"_yst{nb}")
        p.dma("sync", lambda s, nb=nb: s.dma_start(out=ydst[:, nb * 4:(nb + 1) * 4], in_=yo[:]))
    f_yo(); f_hB(); f_hA(); f_we2()
    f_gcol(); f_gsc()

    epilogue_ln_add(p, c, y_d.ap(), xin_ap, xout_ap, uniq=uniq)


WSPEC_L1F = [("fast_b", "L1_ffn_fast_b", "T"),
         ("w_fc", "L1_ffn_w_fc", "T"),
         ("w_proj", "L1_ffn_w_proj", "T"),
         ("u", "L1_ffn_u", "T"),
         ("v", "L1_ffn_v", "T"),
         ("gate", "L1_ffn_gate", None)]


_doc_L2A = """L2_attn sub-block: cosine-sim attention + 2-layer MLP, sandwich step.
xout = xin + LN( block( LN(xin) ) ).
Weights delivered host-side as [din, dout] (original W.T for x@W.T linears).
"""
import numpy as np



SEQ = 2048
NKT = SEQ // 128
NQB = SEQ // 512


def build_l2_attn(p, c, xin_ap, xout_ap, w, pb, pbt0, pbt1, uniq="l2a"):
    aout_d = p.dram(uniq + "_aout", [128, KC, T], BF16)
    h1_d = p.dram(uniq + "_h1", [128, 32, T], BF16)
    y_d = p.dram(uniq + "_y", [T, D], F32)

    lnT, f_lnT = sbuf_tmp(p, uniq + "_lnT", [128, KC, T], BF16)
    prologue_ln(p, c, xin_ap, lnT, pbt0, pbt1, uniq=uniq)

    p.stage(uniq + "_wqkv")
    bq, f_bq = load_bias_cols(p, uniq + "_bq", w["bq"], D)
    bk, f_bk = load_bias_cols(p, uniq + "_bk", w["bk"], D)
    bvb, f_bvb = bias_bcast_row(p, c, uniq + "_bvb", w["bv"], D, pb[0])

    for b in range(2):
        tb = [b * NQB + i for i in range(NQB)]
        qT, f_qT = sbuf_tmp(p, uniq + f"_qT{b}", [128, KC, SEQ], BF16)
        kT, f_kT = sbuf_tmp(p, uniq + f"_kT{b}", [128, KC, SEQ], BF16)
        vtok, f_v = sbuf_tmp(p, uniq + f"_v{b}", [128, NKT, D], BF16)
        p.stage(uniq + f"_wq{b}")
        wq, f_wq = sbuf_tmp(p, uniq + f"_wq{b}", [128, KC, D], BF16)
        p.dma("gpsimd", lambda g, wq=wq: g.dma_start(out=wq[:], in_=w["Wq"].rearrange("(k p) o -> p k o", p=128)))
        proj_fm(p, wq, lnT, qT, D, bias_cols=bq, pbanks=pb[0:4], name=uniq + f"_q{b}",
                tblocks=tb, out_off=b * SEQ)
        f_wq()
        p.stage(uniq + f"_wk{b}")
        wk, f_wk = sbuf_tmp(p, uniq + f"_wk{b}", [128, KC, D], BF16)
        p.dma("gpsimd", lambda g, wk=wk: g.dma_start(out=wk[:], in_=w["Wk"].rearrange("(k p) o -> p k o", p=128)))
        proj_fm(p, wk, lnT, kT, D, bias_cols=bk, pbanks=pb[0:4], name=uniq + f"_k{b}",
                tblocks=tb, out_off=b * SEQ)
        f_wk()
        p.stage(uniq + f"_wv{b}")
        wv, f_wv = sbuf_tmp(p, uniq + f"_wv{b}", [128, KC, D], BF16)
        p.dma("gpsimd", lambda g, wv=wv: g.dma_start(out=wv[:], in_=w["Wv"].rearrange("(k p) o -> p k o", p=128)))
        for jt in range(NKT):
            ti = b * NKT + jt
            p.stage(uniq + f"_v{b}_mm{jt}")
            def vmm(t, ti=ti):
                last = None
                for half in range(2):
                    pbx = pb[4] if half == 0 else pb[5]
                    for k in range(KC):
                        last = t.matmul(pbx[:, 0:512],
                                        lnT[:, k, ti * 128:(ti + 1) * 128],
                                        wv[:, k, half * 512:(half + 1) * 512],
                                        start=(k == 0), stop=(k == KC - 1))
                return last
            p.comp("tensor", vmm, n=2 * KC)
            p.stage(uniq + f"_v{b}_ev{jt}")
            def vev(v, jt=jt):
                v.tensor_tensor(vtok[:, jt, 0:512], pb[4][:, 0:512], bvb[:, 0:512], ALU.add)
                return v.tensor_tensor(vtok[:, jt, 512:1024], pb[5][:, 0:512], bvb[:, 512:1024], ALU.add)
            p.comp("vector", vev, n=2)
        f_wv()

        scr, f_scr = sbuf_tmp(p, uniq + f"_scr{b}", [128, KC, 512], F32)
        for (mat, nm) in ((qT, "q"), (kT, "k")):
            nrm_row, f_nr = sbuf_tmp(p, uniq + f"_nr{nm}{b}", [1, SEQ], F32)
            for nb in range(NQB):
                p.stage(uniq + f"_nsq{nm}{b}_{nb}")
                def nsq(s, mat=mat, nb=nb):
                    last = None
                    for k in range(KC):
                        last = s.activation(scr[:, k, :], mat[:, k, nb * 512:(nb + 1) * 512], AF.Square)
                    return last
                p.comp("scalar", nsq, n=KC)
                p.stage(uniq + f"_nmm{nm}{b}_{nb}")
                def nmm(t):
                    last = None
                    for k in range(KC):
                        last = t.matmul(pb[0][0:1, 0:512], c["ones_colf"][:], scr[:, k, :],
                                        start=(k == 0), stop=(k == KC - 1))
                    return last
                p.comp("tensor", nmm, n=KC)
                p.stage(uniq + f"_nev{nm}{b}_{nb}")
                p.comp("vector", lambda v, nb=nb, nrm_row=nrm_row: v.tensor_copy(
                    nrm_row[:, nb * 512:(nb + 1) * 512], pb[0][0:1, 0:512]))
            p.stage(uniq + f"_nsqr{nm}{b}")
            p.comp("scalar", lambda s, nrm_row=nrm_row: s.activation(nrm_row[:], nrm_row[:], AF.Sqrt))
            p.stage(uniq + f"_nmx{nm}{b}")
            p.comp("vector", lambda v, nrm_row=nrm_row: v.tensor_single_scalar(
                nrm_row[:], nrm_row[:], 1e-12, ALU.max))
            p.stage(uniq + f"_nrc{nm}{b}")
            p.comp("vector", lambda v, nrm_row=nrm_row: v.reciprocal(nrm_row[:], nrm_row[:]))
            for nb in range(NQB):
                p.stage(uniq + f"_nbc{nm}{b}_{nb}")
                p.comp("tensor", lambda t, nb=nb, nrm_row=nrm_row: t.matmul(
                    pb[1][:, 0:512], c["ones_rowf"][0:1, 0:128],
                    nrm_row[:, nb * 512:(nb + 1) * 512], start=True, stop=True))
                p.stage(uniq + f"_nap{nm}{b}_{nb}")
                def nap(v, mat=mat, nb=nb):
                    last = None
                    for k in range(KC):
                        sl = mat[:, k, nb * 512:(nb + 1) * 512]
                        last = v.tensor_tensor(sl, sl, pb[1][:, 0:512], ALU.mult)
                    return last
                p.comp("vector", nap, n=KC)
            f_nr()
        f_scr()

        E, f_E = sbuf_tmp(p, uniq + f"_E{b}", [128, NKT, 512], BF16)
        den_row, f_dr = sbuf_tmp(p, uniq + f"_dr{b}", [1, 512], F32)
        rden_bc, f_rb = sbuf_tmp(p, uniq + f"_rden{b}", [128, 512], F32)
        ao, f_ao = sbuf_tmp(p, uniq + f"_ao{b}", [128, KC, 512], BF16)
        for qb in range(NQB):
            q0 = qb * 512
            for kt in range(NKT):
                pbx = pb[2 + kt % 2]
                p.stage(uniq + f"_s{b}_{qb}_{kt}")
                def smm(t, kt=kt, pbx=pbx, q0=q0, kT=kT, qT=qT):
                    last = None
                    for k in range(KC):
                        last = t.matmul(pbx[:, 0:512], kT[:, k, kt * 128:(kt + 1) * 128],
                                        qT[:, k, q0:q0 + 512], start=(k == 0), stop=(k == KC - 1))
                    return last
                p.comp("tensor", smm, n=KC)
                p.stage(uniq + f"_e{b}_{qb}_{kt}")
                p.comp("scalar", lambda s, kt=kt, pbx=pbx, E=E: s.activation(
                    E[:, kt, :], pbx[:, 0:512], AF.Exp))
            p.stage(uniq + f"_den{b}_{qb}")
            def denmm(t, E=E):
                last = None
                for kt in range(NKT):
                    last = t.matmul(pb[0][0:1, 0:512], c["ones_colb"][:], E[:, kt, :],
                                    start=(kt == 0), stop=(kt == NKT - 1))
                return last
            p.comp("tensor", denmm, n=NKT)
            p.stage(uniq + f"_denev{b}_{qb}")
            p.comp("vector", lambda v, den_row=den_row: v.tensor_copy(den_row[:], pb[0][0:1, 0:512]))
            p.stage(uniq + f"_denrc{b}_{qb}")
            p.comp("vector", lambda v, den_row=den_row: v.reciprocal(den_row[:], den_row[:]))
            p.stage(uniq + f"_denbc{b}_{qb}")
            p.comp("tensor", lambda t, den_row=den_row: t.matmul(
                pb[1][:, 0:512], c["ones_rowf"][0:1, 0:128], den_row[:], start=True, stop=True))
            p.stage(uniq + f"_denbcev{b}_{qb}")
            p.comp("vector", lambda v, rden_bc=rden_bc: v.tensor_copy(rden_bc[:], pb[1][:, 0:512]))
            for m in range(KC):
                pbx = pb[2 + m % 2]
                p.stage(uniq + f"_o{b}_{qb}_{m}")
                def omm(t, m=m, pbx=pbx, vtok=vtok, E=E):
                    last = None
                    for kt in range(NKT):
                        last = t.matmul(pbx[:, 0:512], vtok[:, kt, m * 128:(m + 1) * 128],
                                        E[:, kt, :], start=(kt == 0), stop=(kt == NKT - 1))
                    return last
                p.comp("tensor", omm, n=NKT)
                p.stage(uniq + f"_oev{b}_{qb}_{m}")
                p.comp("vector", lambda v, m=m, pbx=pbx, ao=ao, rden_bc=rden_bc: v.tensor_tensor(
                    ao[:, m, :], pbx[:, 0:512], rden_bc[:], ALU.mult))
            p.stage(uniq + f"_aodma{b}_{qb}")
            off = (b * NQB + qb) * 512
            p.dma("sync", lambda s, off=off, ao=ao: s.dma_start(
                out=aout_d.ap()[:, :, off:off + 512], in_=ao[:]))
        f_ao(); f_rb(); f_dr(); f_E()
        f_v(); f_kT(); f_qT()
    f_lnT()

    f_bvb(); f_bk(); f_bq()
    p.stage(uniq + "_w1ld")
    w1, f_w1 = sbuf_tmp(p, uniq + "_w1", [128, KC, D], BF16)
    p.dma("gpsimd", lambda g: g.dma_start(out=w1[:], in_=w["W1"].rearrange("(k p) o -> p k o", p=128)))
    b1, f_b1 = load_bias_cols(p, uniq + "_b1", w["b1"], D)
    ain, f_ain = sbuf_tmp(p, uniq + "_ain", [128, KC, 512], BF16)
    ain2, f_ain2 = sbuf_tmp(p, uniq + "_ain2", [128, KC, 512], BF16)
    h1o, f_h1o = sbuf_tmp(p, uniq + "_h1o", [128, KC, 512], BF16)
    p.stage(uniq + "_mlp1_ld0")
    p.dma("sync", lambda s: s.dma_start(out=ain[:], in_=aout_d.ap()[:, :, 0:512]))
    for nb in range(NB):
        src = ain if nb % 2 == 0 else ain2
        p.stage(uniq + f"_mlp1pre_{nb}")
        if nb + 1 < NB:
            nxt = ain2 if nb % 2 == 0 else ain
            p.dma("sync", lambda s, nb=nb, nxt=nxt: s.dma_start(
                out=nxt[:], in_=aout_d.ap()[:, :, (nb + 1) * 512:(nb + 2) * 512]))
        proj_fm(p, w1, src, h1o, D, bias_cols=b1, act=AF.Relu, pbanks=pb[0:4],
                name=uniq + f"_m1_{nb}", tblocks=[nb], rhs_off=nb * 512, out_off=nb * 512)
        p.stage(uniq + f"_mlp1st_{nb}")
        p.dma("sync", lambda s, nb=nb: s.dma_start(
            out=h1_d.ap()[:, 0:KC, nb * 512:(nb + 1) * 512], in_=h1o[:]))
    f_h1o(); f_ain2(); f_ain(); f_w1()

    f_b1()
    p.stage(uniq + "_w23ld")
    w2, f_w2 = sbuf_tmp(p, uniq + "_w2", [128, KC, 4096], BF16)
    p.dma("gpsimd", lambda g: g.dma_start(out=w2[:], in_=w["W2"].rearrange("(k p) o -> p k o", p=128)))
    w3, f_w3 = sbuf_tmp(p, uniq + "_w3", [128, 32, D], BF16)
    p.dma("gpsimd", lambda g: g.dma_start(out=w3[:], in_=w["W3"].rearrange("(k p) o -> p k o", p=128)))
    b2, f_b2 = load_bias_cols(p, uniq + "_b2", w["b2"], 4096)
    b3b, f_b3b = bias_bcast_row(p, c, uniq + "_b3b", w["b3"], D, pb[0])
    h1in, f_h1in = sbuf_tmp(p, uniq + "_h1in", [128, KC, 512], BF16)
    h2T, f_h2T = sbuf_tmp(p, uniq + "_h2T", [128, 32, 512], BF16)
    yo, f_yo = sbuf_tmp(p, uniq + "_yo", [128, 4, D], F32)
    ydst = y_d.ap().rearrange("(n p) d -> p n d", p=128)
    for nb in range(NB):
        p.stage(uniq + f"_mlp2ld_{nb}")
        p.dma("sync", lambda s, nb=nb: s.dma_start(
            out=h1in[:], in_=h1_d.ap()[:, 0:KC, nb * 512:(nb + 1) * 512]))
        proj_fm(p, w2, h1in, h2T, 4096, bias_cols=b2, act=AF.Relu, pbanks=pb[0:4],
                name=uniq + f"_m2_{nb}", tblocks=[0], rhs_off=0, out_off=0)
        for jt in range(4):
            p.stage(uniq + f"_mlp3_{nb}_{jt}")
            def mm3(t, jt=jt, h2T=h2T, w3=w3):
                last = None
                for half in range(2):
                    pbx = pb[4 + half]
                    for k in range(32):
                        last = t.matmul(pbx[:, 0:512], h2T[:, k, jt * 128:(jt + 1) * 128],
                                        w3[:, k, half * 512:(half + 1) * 512],
                                        start=(k == 0), stop=(k == 31))
                return last
            p.comp("tensor", mm3, n=64)
            p.stage(uniq + f"_mlp3ev_{nb}_{jt}")
            def yev(v, jt=jt, yo=yo):
                v.tensor_tensor(yo[:, jt, 0:512], pb[4][:, 0:512], b3b[:, 0:512], ALU.add)
                return v.tensor_tensor(yo[:, jt, 512:1024], pb[5][:, 0:512], b3b[:, 512:1024], ALU.add)
            p.comp("vector", yev, n=2)
        p.stage(uniq + f"_mlp3st_{nb}")
        p.dma("sync", lambda s, nb=nb: s.dma_start(out=ydst[:, nb * 4:(nb + 1) * 4], in_=yo[:]))
    f_yo(); f_h2T(); f_h1in(); f_w3(); f_w2(); f_b3b(); f_b2()

    epilogue_ln_add(p, c, y_d.ap(), xin_ap, xout_ap, uniq=uniq)


WSPEC_L2A = [("Wq", "L2_attn_Wq", "T"), ("bq", "L2_attn_bq", None),
         ("Wk", "L2_attn_Wk", "T"), ("bk", "L2_attn_bk", None),
         ("Wv", "L2_attn_Wv", "T"), ("bv", "L2_attn_bv", None),
         ("W1", "L2_attn_W1", "T"), ("b1", "L2_attn_b1", None),
         ("W2", "L2_attn_W2", "T"), ("b2", "L2_attn_b2", None),
         ("W3", "L2_attn_W3", "T"), ("b3", "L2_attn_b3", None)]


_doc_L2F = """L2_ffn: twist coupling cycles sandwich step."""
import numpy as np



SEQ = 2048
PERM = [4, 5, 6, 7, 0, 1, 2, 3]
SGN = [1.0, 1.0, -1.0, -1.0, -1.0, -1.0, 1.0, 1.0]


def build_l2_ffn(p, c, xin_ap, xout_ap, w, pb, pbt0, pbt1, uniq="l2f"):
    y_d = p.dram(uniq + "_y", [T, D], F32)

    outA, f_oA = sbuf_tmp(p, uniq + "_outA", [128, KC, T], BF16)
    prologue_ln(p, c, xin_ap, outA, pbt0, pbt1, uniq=uniq)
    outB, f_oB = sbuf_tmp(p, uniq + "_outB", [128, KC, T], BF16)

    p.stage(uniq + "_wld")
    wcn1, f_w1 = sbuf_tmp(p, uniq + "_wcn1", [128, KC, D], BF16)
    p.dma("gpsimd", lambda g: g.dma_start(out=wcn1[:], in_=w["cn_w1"].rearrange("(k p) o -> p k o", p=128)))
    wcn2, f_w2 = sbuf_tmp(p, uniq + "_wcn2", [128, KC, 512], BF16)
    p.dma("gpsimd", lambda g: g.dma_start(out=wcn2[:], in_=w["cn_w2"].rearrange("(k p) o -> p k o", p=128)))
    wcn3, f_w3 = sbuf_tmp(p, uniq + "_wcn3", [128, 4, 256], BF16)
    p.dma("gpsimd", lambda g: g.dma_start(out=wcn3[:], in_=w["cn_w3"].rearrange("(k p) o -> p k o", p=128)))
    wcn4, f_w4 = sbuf_tmp(p, uniq + "_wcn4", [128, 2, 1], BF16)
    p.dma("gpsimd", lambda g: g.dma_start(out=wcn4[:], in_=w["cn_w4"].rearrange("(k p) o -> p k o", p=128), allow_slow_non_contiguous=True))
    wgc1, f_g1 = sbuf_tmp(p, uniq + "_wgc1", [128, KC, 512], BF16)
    p.dma("gpsimd", lambda g: g.dma_start(out=wgc1[:], in_=w["gc_w1"].rearrange("(k p) o -> p k o", p=128)))
    wgc2, f_g2 = sbuf_tmp(p, uniq + "_wgc2", [128, 4, 256], BF16)
    p.dma("gpsimd", lambda g: g.dma_start(out=wgc2[:], in_=w["gc_w2"].rearrange("(k p) o -> p k o", p=128)))
    wgc3, f_g3 = sbuf_tmp(p, uniq + "_wgc3", [128, 2, 1], BF16)
    p.dma("gpsimd", lambda g: g.dma_start(out=wgc3[:], in_=w["gc_w3"].rearrange("(k p) o -> p k o", p=128), allow_slow_non_contiguous=True))
    bcn1, f_b1 = load_bias_cols(p, uniq + "_bcn1", w["cn_b1"], D)
    bcn2, f_b2 = load_bias_cols(p, uniq + "_bcn2", w["cn_b2"], 512)
    bcn3, f_b3 = load_bias_cols(p, uniq + "_bcn3", w["cn_b3"], 256)
    sm, f_sm = sbuf_tmp(p, uniq + "_sm", [1, 8], F32, side="right")
    p.dma("sync", lambda s: s.dma_start(out=sm[0:1, 0:1], in_=w["cn_b4"].rearrange("(a d) -> a d", a=1)))
    p.dma("sync", lambda s: s.dma_start(out=sm[0:1, 1:2], in_=w["gc_b3"].rearrange("(a d) -> a d", a=1)))
    p.dma("sync", lambda s: s.dma_start(out=sm[0:1, 2:3], in_=w["adaptive_range"].rearrange("(a d) -> a d", a=1)))
    bgc1, f_bg1 = load_bias_cols(p, uniq + "_bgc1", w["gc_b1"], 512)
    bgc2, f_bg2 = load_bias_cols(p, uniq + "_bgc2", w["gc_b2"], 256)

    gcT, f_gc = sbuf_tmp(p, uniq + "_gcT", [128, KC, 2], F32)
    gcTb, f_gcb = sbuf_tmp(p, uniq + "_gcTb", [128, KC, 2], BF16)
    p.stage(uniq + "_gc")
    def gcred(v):
        last = None
        for b in range(2):
            for k in range(KC):
                last = v.reduce_sum(gcT[:, k, b:b + 1], outA[:, k, b * SEQ:(b + 1) * SEQ], axis=AX.X)
        return last
    p.comp("vector", gcred, n=2 * KC)
    p.stage(uniq + "_gcm")
    p.comp("vector", lambda v: v.tensor_single_scalar(gcTb[:], gcT[:], 1.0 / SEQ, ALU.mult))
    g1o, f_g1o = sbuf_tmp(p, uniq + "_g1o", [128, 4, 2], BF16)
    g2o, f_g2o = sbuf_tmp(p, uniq + "_g2o", [128, 2, 2], BF16)
    gf, f_gf = sbuf_tmp(p, uniq + "_gf", [1, 4], F32)
    for m in range(4):
        p.stage(uniq + f"_gn1_{m}")
        def g1mm(t, m=m):
            last = None
            for k in range(KC):
                last = t.matmul(pb[0][:, 0:2], wgc1[:, k, m * 128:(m + 1) * 128],
                                gcTb[:, k, :], start=(k == 0), stop=(k == KC - 1))
            return last
        p.comp("tensor", g1mm, n=KC)
        p.stage(uniq + f"_gn1e_{m}")
        p.comp("scalar", lambda s, m=m: s.activation(g1o[:, m, :], pb[0][:, 0:2], AF.Gelu,
                                                     bias=bgc1[:, m:m + 1]))
    for m in range(2):
        p.stage(uniq + f"_gn2_{m}")
        def g2mm(t, m=m):
            last = None
            for k in range(4):
                last = t.matmul(pb[0][:, 0:2], wgc2[:, k, m * 128:(m + 1) * 128],
                                g1o[:, k, :], start=(k == 0), stop=(k == 3))
            return last
        p.comp("tensor", g2mm, n=4)
        p.stage(uniq + f"_gn2e_{m}")
        p.comp("scalar", lambda s, m=m: s.activation(g2o[:, m, :], pb[0][:, 0:2], AF.Gelu,
                                                     bias=bgc2[:, m:m + 1]))
    p.stage(uniq + "_gn3")
    def g3mm(t):
        last = None
        for k in range(2):
            last = t.matmul(pb[0][0:1, 0:2], wgc3[:, k, 0:1], g2o[:, k, :],
                            start=(k == 0), stop=(k == 1))
        return last
    p.comp("tensor", g3mm, n=2)
    p.stage(uniq + "_gn3e")
    p.comp("scalar", lambda s: s.activation(gf[0:1, 0:2], pb[0][0:1, 0:2], AF.Sigmoid,
                                            bias=sm[0:1, 1:2]))
    p.stage(uniq + "_gf7")
    p.comp("vector", lambda v: v.tensor_single_scalar(gf[0:1, 2:4], gf[0:1, 0:2], 0.7, ALU.mult))

    h1T, f_h1T = sbuf_tmp(p, uniq + "_h1T", [128, KC, 512], BF16)
    h2T, f_h2T = sbuf_tmp(p, uniq + "_h2T", [128, 4, 512], BF16)
    h3T, f_h3T = sbuf_tmp(p, uniq + "_h3T", [128, 2, 512], BF16)
    cpl, f_cpl = sbuf_tmp(p, uniq + "_cpl", [1, 512], F32)
    cplb, f_cplb = sbuf_tmp(p, uniq + "_cplb", [128, 512], F32)
    tw, f_tw = sbuf_tmp(p, uniq + "_tw", [128, 512], F32)
    for cyc in range(3):
        src_m = outA if cyc % 2 == 0 else outB
        dst_m = outB if cyc % 2 == 0 else outA
        for nb in range(NB):
            b = nb // 4
            q0 = nb * 512
            proj_fm(p, wcn1, src_m, h1T, D, bias_cols=bcn1, act=AF.Gelu, pbanks=pb[0:4],
                    name=uniq + f"_c{cyc}n{nb}a", tblocks=[nb], rhs_off=q0, out_off=q0)
            proj_fm(p, wcn2, h1T, h2T, 512, bias_cols=bcn2, act=AF.Gelu, pbanks=pb[0:4],
                    name=uniq + f"_c{cyc}n{nb}b", tblocks=[0])
            proj_fm(p, wcn3, h2T, h3T, 256, din=512, bias_cols=bcn3, act=AF.Gelu, pbanks=pb[0:4],
                    name=uniq + f"_c{cyc}n{nb}c", tblocks=[0])
            p.stage(uniq + f"_c{cyc}n{nb}d")
            def c4mm(t):
                last = None
                for k in range(2):
                    last = t.matmul(pb[4][0:1, 0:512], wcn4[:, k, 0:1], h3T[:, k, :],
                                    start=(k == 0), stop=(k == 1))
                return last
            p.comp("tensor", c4mm, n=2)
            p.stage(uniq + f"_c{cyc}n{nb}tf")
            p.comp("scalar", lambda s: s.activation(cpl[:], pb[4][0:1, 0:512], AF.Sigmoid,
                                                    bias=sm[0:1, 0:1]))
            p.stage(uniq + f"_c{cyc}n{nb}cp1")
            p.comp("vector", lambda v, b=b: v.tensor_scalar(
                cpl[:], cpl[:], 0.3, gf[0:1, 2 + b:3 + b], ALU.mult, ALU.add))
            p.stage(uniq + f"_c{cyc}n{nb}cp2")
            p.comp("vector", lambda v: v.tensor_scalar(
                cpl[:], cpl[:], -0.5, 2.0, ALU.add, ALU.mult))
            p.stage(uniq + f"_c{cyc}n{nb}cp3")
            p.comp("vector", lambda v: v.tensor_scalar(
                cpl[:], cpl[:], sm[0:1, 2:3], 0.1, ALU.mult, ALU.add))
            p.stage(uniq + f"_c{cyc}n{nb}cb")
            p.comp("tensor", lambda t: t.matmul(pb[5][:, 0:512], c["ones_rowf"][0:1, 0:128],
                                                cpl[:], start=True, stop=True))
            p.stage(uniq + f"_c{cyc}n{nb}cbe")
            p.comp("vector", lambda v: v.tensor_copy(cplb[:], pb[5][:, 0:512]))
            for ch in range(KC):
                p.stage(uniq + f"_c{cyc}n{nb}tw{ch}")
                p.comp("vector", lambda v, ch=ch, q0=q0, src_m=src_m: v.tensor_tensor(
                    tw[:], src_m[:, PERM[ch], q0:q0 + 512], cplb[:], ALU.mult))
                p.stage(uniq + f"_c{cyc}n{nb}ad{ch}")
                p.comp("vector", lambda v, ch=ch, q0=q0, src_m=src_m, dst_m=dst_m: v.scalar_tensor_tensor(
                    dst_m[:, ch, q0:q0 + 512], tw[:], SGN[ch], src_m[:, ch, q0:q0 + 512],
                    ALU.mult, ALU.add))
    f_tw(); f_cplb(); f_cpl(); f_h3T(); f_h2T(); f_h1T()

    transpose_fm_to_tok(p, c, outB, y_d.ap(), pbt0, pbt1, uniq=uniq)
    f_gf(); f_g2o(); f_g1o(); f_gcb(); f_gc()
    f_bg2(); f_bg1(); f_sm(); f_b3(); f_b2(); f_b1()
    f_g3(); f_g2(); f_g1(); f_w4(); f_w3(); f_w2(); f_w1()
    f_oB(); f_oA()
    epilogue_ln_add(p, c, y_d.ap(), xin_ap, xout_ap, uniq=uniq)


WSPEC_L2F = ([("cn_w%d" % i, "L2_ffn_cn_w%d" % i, None) for i in (1, 2, 3, 4)]
         + [("cn_b%d" % i, "L2_ffn_cn_b%d" % i, None) for i in (1, 2, 3, 4)]
         + [("gc_w%d" % i, "L2_ffn_gc_w%d" % i, None) for i in (1, 2, 3)]
         + [("gc_b%d" % i, "L2_ffn_gc_b%d" % i, None) for i in (1, 2, 3)]
         + [("adaptive_range", "L2_ffn_adaptive_range", None)])


_doc_L3A = """L3_attn: GQA 16q/4kv heads dk=64, interleaved RoPE, bidirectional sliding window
|dq-dk|<=128, per-head sink logits, internal residuals + rmsnorms, then dense top-4-of-8
MoE (gate/up clip + silu-ish glu).  Host delivers permuted flattened weights:
q/k columns reordered (head order QPERM for q, interleave-split within head), plus
"swap" copies for the rope cross terms; Wo rows in matching order."""
import numpy as np
import os



SEQ = 2048
NTS = SEQ // 128
SCALE = 64.0 ** (-0.25)
LN15 = float(np.log(150000.0))
TWOPI = float(2.0 * np.pi)
QPERM = [0, 4, 1, 5, 2, 6, 3, 7, 8, 12, 9, 13, 10, 14, 11, 15]
P64 = np.concatenate([np.arange(0, 64, 2), np.arange(1, 64, 2)])
SW64 = np.concatenate([np.arange(32, 64), np.arange(0, 32)])


def _qperm_cols(swap):
    idx = []
    for i in range(16):
        u = QPERM[i]
        loc = P64[SW64] if swap else P64
        idx.extend((u * 64 + loc).tolist())
    return np.array(idx)


def _kperm_cols(swap):
    idx = []
    for g in range(4):
        loc = P64[SW64] if swap else P64
        idx.extend((g * 64 + loc).tolist())
    return np.array(idx)


def _wq(a, swap=False):
    return a.reshape(1024, 1024)[:, _qperm_cols(swap)]


def _bq(a, swap=False):
    return a.reshape(1024)[_qperm_cols(swap)]


def _wk(a, swap=False):
    return a.reshape(1024, 256)[:, _kperm_cols(swap)]


def _bk(a, swap=False):
    return a.reshape(256)[_kperm_cols(swap)]


def _wo(a):
    rows = []
    for i in range(16):
        rows.extend((QPERM[i] * 64 + np.arange(64)).tolist())
    return a.reshape(1024, 1024)[np.array(rows), :]


def build_l3_attn(p, c, xin_ap, xout_ap, w, pb, pbt0, pbt1, uniq="l3a"):
    PH = int(os.environ.get("L3PH", "99"))
    AT = int(os.environ.get("L3AT", "9"))
    B64 = int(os.environ.get("L3B64", "1"))
    hid_d = p.dram(uniq + "_hid", [T, D], BF16)
    oT_d = p.dram(uniq + "_oT", [128, KC, T], BF16)
    h1_d = p.dram(uniq + "_h1", [T, D], F32)
    rsd2_d = p.dram(uniq + "_rs2", [NT, 128], F32)
    coef_d = p.dram(uniq + "_coef", [8, T], F32)
    gated_d = p.dram(uniq + "_gated", [8, 128, 8, T], BF16)
    h2_d = p.dram(uniq + "_h2", [T, D], F32)

    mkP, f_mkP = sbuf_tmp(p, uniq + "_mkP", [128, 512], BF16, side="right")
    mkN, f_mkN = sbuf_tmp(p, uniq + "_mkN", [128, 512], BF16, side="right")
    esk, f_esk = sbuf_tmp(p, uniq + "_esk", [128, 16], F32, side="right")
    srow, f_srow = sbuf_tmp(p, uniq + "_srow", [1, 16], F32, side="right")
    iota8t, f_i8 = sbuf_tmp(p, uniq + "_i8t", [128, 8], F32, side="right")
    sgsc, f_sg = sbuf_tmp(p, uniq + "_sgsc", [128, 1], F32, side="right")
    pmcol, f_pmc = sbuf_tmp(p, uniq + "_pmc", [128, 2], F32, side="right")
    invfr, f_invf = sbuf_tmp(p, uniq + "_invf", [1, 128], F32, side="right")
    posrow, f_pos = sbuf_tmp(p, uniq + "_pos", [1, SEQ], F32, side="right")
    p.stage(uniq + "_c0")
    def mk0(v):
        v.tensor_scalar(mkP[:, 0:128], c["iota_row"][:], c["iota_col"][:, 0:1], None, ALU.is_le)
        return v.tensor_scalar(mkN[:, 0:128], c["iota_row"][:], c["iota_col"][:, 0:1], None, ALU.is_ge)
    p.comp("vector", mk0, n=2)
    p.comp("gpsimd", lambda g: g.iota(iota8t[:], [[1, 8]], channel_multiplier=0,
                                      allow_small_or_imprecise_dtypes=True))
    for j in range(4):
        p.comp("gpsimd", lambda g, j=j: g.iota(posrow[:, j * 512:(j + 1) * 512], [[1, 512]],
                                               base=j * 512, channel_multiplier=0,
                                               allow_small_or_imprecise_dtypes=True))
    p.dma("sync", lambda s: s.dma_start(out=srow[:], in_=w["sinks"].rearrange("(a d) -> a d", a=1)))
    p.stage(uniq + "_c1")
    def mk1(v):
        last = None
        for j in range(1, 4):
            v.tensor_copy(mkP[:, j * 128:(j + 1) * 128], mkP[:, 0:128])
            last = v.tensor_copy(mkN[:, j * 128:(j + 1) * 128], mkN[:, 0:128])
        return last
    p.comp("vector", mk1, n=6)
    p.comp("scalar", lambda s: s.activation(srow[:], srow[:], AF.Exp))
    p.comp("gpsimd", lambda g: g.tensor_scalar(pmcol[:, 0:1], c["iota_col"][:], 64.0, None, ALU.is_ge))
    p.stage(uniq + "_c2")
    p.comp("vector", lambda v: v.scalar_tensor_tensor(pmcol[:, 0:1], pmcol[:, 0:1], -64.0,
                                                      c["iota_col"][:], ALU.mult, ALU.add))
    p.comp("tensor", lambda t: t.matmul(pb[2][0:128, 0:16], c["ones_rowf"][0:1, :], srow[:],
                                        start=True, stop=True))
    p.stage(uniq + "_c3")
    p.comp("vector", lambda v: v.tensor_scalar(pmcol[:, 1:2], pmcol[:, 0:1], 32.0, None, ALU.is_ge))
    p.comp("scalar", lambda s: s.copy(esk[:], pb[2][:, 0:16]))
    p.stage(uniq + "_c4")
    p.comp("vector", lambda v: v.scalar_tensor_tensor(pmcol[:, 1:2], pmcol[:, 1:2], -32.0,
                                                      pmcol[:, 0:1], ALU.mult, ALU.add))
    p.stage(uniq + "_c5")
    p.comp("vector", lambda v: v.tensor_scalar(sgsc[:], pmcol[:, 0:1], 32.0, None, ALU.is_lt))
    p.stage(uniq + "_c6")
    p.comp("vector", lambda v: v.tensor_scalar(sgsc[:], sgsc[:], -2.0, 1.0, ALU.mult, ALU.add))
    p.stage(uniq + "_c7")
    p.comp("vector", lambda v: v.tensor_scalar(sgsc[:], sgsc[:], SCALE, None, ALU.mult))
    p.comp("scalar", lambda s: s.activation(pmcol[:, 1:2], pmcol[:, 1:2], AF.Exp, scale=-LN15 / 32.0))
    p.stage(uniq + "_c8")
    p.comp("tensor", lambda t: t.transpose(pb[3][0:1, 0:128], pmcol[:, 1:2], c["idf"][:]))
    p.stage(uniq + "_c9")
    p.comp("vector", lambda v: v.tensor_copy(invfr[:], pb[3][0:1, 0:128]))

    bqc, f_bqc = load_bias_cols(p, uniq + "_bqc", w["bq"], 1024)
    bqsc, f_bqsc = load_bias_cols(p, uniq + "_bqsc", w["bqs"], 1024)
    bkc, f_bkc = sbuf_tmp(p, uniq + "_bkc", [128, 2], F32, side="right")
    bksc, f_bksc = sbuf_tmp(p, uniq + "_bksc", [128, 2], F32, side="right")
    p.dma("sync", lambda s: s.dma_start(out=bkc[:], in_=w["bk"].rearrange("(k p) -> p k", p=128),
                                        allow_slow_non_contiguous=True))
    p.dma("sync", lambda s: s.dma_start(out=bksc[:], in_=w["bks"].rearrange("(k p) -> p k", p=128),
                                        allow_slow_non_contiguous=True))
    bvb, f_bvb = bias_bcast_row(p, c, uniq + "_bvb", w["bv"], 256, pb[0])
    ns1c, f_ns1 = load_bias_cols(p, uniq + "_ns1", w["ns1"], 1024)
    ns2c, f_ns2 = load_bias_cols(p, uniq + "_ns2", w["ns2"], 1024)

    lnT, f_lnT = sbuf_tmp(p, uniq + "_lnT", [128, KC, T], BF16, side="right")
    prologue_ln(p, c, xin_ap, lnT, pbt0, pbt1, ln_tok_dram=hid_d.ap(), uniq=uniq)

    sqs, f_sqs = sbuf_tmp(p, uniq + "_sqs", [128, KC, 512], F32)
    rs1, f_rs1 = sbuf_tmp(p, uniq + "_rs1", [1, T], F32)
    for nb in range(NB):
        p.stage(uniq + f"_s1a{nb}")
        def sqf(s, nb=nb):
            last = None
            for k in range(KC):
                last = s.activation(sqs[:, k, :], lnT[:, k, nb * 512:(nb + 1) * 512], AF.Square)
            return last
        p.comp("scalar", sqf, n=KC)
        p.stage(uniq + f"_s1b{nb}")
        def smm(t):
            last = None
            for k in range(KC):
                last = t.matmul(pb[1][0:1, 0:512], c["ones_colf"][:], sqs[:, k, :],
                                start=(k == 0), stop=(k == KC - 1))
            return last
        p.comp("tensor", smm, n=KC)
        p.stage(uniq + f"_s1c{nb}")
        p.comp("vector", lambda v, nb=nb: v.tensor_scalar(
            rs1[0:1, nb * 512:(nb + 1) * 512], pb[1][0:1, 0:512], 1.0 / D, 1e-5, ALU.mult, ALU.add))
    p.stage(uniq + "_s1d")
    p.comp("scalar", lambda s: s.activation(rs1[:], rs1[:], AF.Sqrt))
    p.stage(uniq + "_s1e")
    p.comp("vector", lambda v: v.reciprocal(rs1[:], rs1[:]))
    p.stage(uniq + "_s1f")
    def nsap(v):
        last = None
        for k in range(KC):
            last = v.tensor_scalar(lnT[:, k, :], lnT[:, k, :], ns1c[:, k:k + 1], None, ALU.mult)
        return last
    p.comp("vector", nsap, n=KC)
    for nb in range(NB):
        p.stage(uniq + f"_s1g{nb}")
        p.comp("tensor", lambda t, nb=nb: t.matmul(pb[0][:, 0:512], c["ones_rowf"][0:1, :],
                                                   rs1[0:1, nb * 512:(nb + 1) * 512],
                                                   start=True, stop=True))
        p.stage(uniq + f"_s1h{nb}")
        def rap(v, nb=nb):
            last = None
            for k in range(KC):
                last = v.tensor_tensor(lnT[:, k, nb * 512:(nb + 1) * 512],
                                       lnT[:, k, nb * 512:(nb + 1) * 512], pb[0][:, 0:512], ALU.mult)
            return last
        p.comp("vector", rap, n=KC)
    f_rs1(); f_sqs()

    ccc, f_ccc = sbuf_tmp(p, uniq + "_ccc", [128, SEQ], BF16)
    ssg, f_ssg = sbuf_tmp(p, uniq + "_ssg", [128, SEQ], BF16)
    ang, f_ang = sbuf_tmp(p, uniq + "_ang", [128, SEQ], F32)
    wk1, f_wk1 = sbuf_tmp(p, uniq + "_wk1", [128, SEQ], F32)
    wki, f_wki = sbuf_tmp(p, uniq + "_wki", [128, SEQ], I32)
    p.stage(uniq + "_r0")
    def angm(t):
        last = None
        for j in range(4):
            last = t.matmul(pb[j % 2][:, 0:512], invfr[0:1, :], posrow[0:1, j * 512:(j + 1) * 512],
                            start=True, stop=True)
        return last
    p.comp("tensor", angm, n=4)
    p.stage(uniq + "_r1")
    def angc(v):
        last = None
        for j in range(4):
            last = v.tensor_copy(ang[:, j * 512:(j + 1) * 512], pb[j % 2][:, 0:512])
        return last
    p.comp("vector", angc, n=4)
    for (nm, shift, dst, sc_ap, sc_imm) in (("sin", 0.0, ssg, sgsc, None), ("cos", np.pi / 2, ccc, None, SCALE)):
        u2 = uniq + "_r" + nm
        p.stage(u2 + "a")
        p.comp("vector", lambda v, shift=shift: v.tensor_scalar(wk1[:], ang[:], 1.0 / TWOPI,
                                                                shift / TWOPI, ALU.mult, ALU.add))
        p.stage(u2 + "b")
        p.comp("vector", lambda v: v.tensor_copy(wki[:], wk1[:]))
        p.stage(u2 + "c")
        p.comp("vector", lambda v: v.tensor_copy(wk1[:], wki[:]))
        p.stage(u2 + "d")
        if shift == 0.0:
            p.comp("vector", lambda v: v.scalar_tensor_tensor(wk1[:], wk1[:], -TWOPI, ang[:],
                                                              ALU.mult, ALU.add))
        else:
            p.comp("vector", lambda v, shift=shift: v.tensor_scalar(wk1[:], wk1[:], -TWOPI, shift,
                                                                    ALU.mult, ALU.add))
            p.stage(u2 + "d2")
            p.comp("vector", lambda v: v.tensor_tensor(wk1[:], wk1[:], ang[:], ALU.add))
        p.stage(u2 + "e")
        p.comp("scalar", lambda s: s.activation(wk1[:], wk1[:], AF.Sin))
        p.stage(u2 + "f")
        if sc_ap is not None:
            p.comp("vector", lambda v, dst=dst, sc_ap=sc_ap: v.tensor_scalar(
                dst[:], wk1[:], sc_ap[:, 0:1], None, ALU.mult))
        else:
            p.comp("vector", lambda v, dst=dst, sc_imm=sc_imm: v.tensor_scalar(
                dst[:], wk1[:], sc_imm, None, ALU.mult))
    f_wki(); f_wk1(); f_ang()

    qrop, f_qrop = sbuf_tmp(p, uniq + "_qrop", [128, KC, T], BF16)
    krop, f_krop = sbuf_tmp(p, uniq + "_krop", [128, 2, T], BF16)
    qa, f_qa = sbuf_tmp(p, uniq + "_qa", [128, 4, 512], BF16)
    qb, f_qb = sbuf_tmp(p, uniq + "_qb", [128, 4, 512], BF16)
    tA, f_tA = sbuf_tmp(p, uniq + "_tA", [128, 512], F32)
    tB, f_tB = sbuf_tmp(p, uniq + "_tB", [128, 512], F32)
    jobs = [("q0", "Wq", "Wqs", bqc[:, 0:4], bqsc[:, 0:4], qrop, 4, 0),
            ("q1", "Wq", "Wqs", bqc[:, 4:8], bqsc[:, 4:8], qrop, 4, 4),
            ("k", "Wk", "Wks", bkc, bksc, krop, 2, 0)]
    for (tag, wk_, wks_, bc, bcs, dst, MC, mo) in jobs:
        wA, f_wA = sbuf_tmp(p, uniq + f"_w{tag}A", [128, KC, MC * 128], BF16)
        wB, f_wB = sbuf_tmp(p, uniq + f"_w{tag}B", [128, KC, MC * 128], BF16)
        p.stage(uniq + f"_wl{tag}")
        p.dma("gpsimd", lambda g, wA=wA, wk_=wk_, mo=mo, MC=MC: g.dma_start(
            out=wA[:], in_=w[wk_].rearrange("(k p) o -> p k o", p=128)[:, :, mo * 128:(mo + MC) * 128]))
        p.dma("gpsimd", lambda g, wB=wB, wks_=wks_, mo=mo, MC=MC: g.dma_start(
            out=wB[:], in_=w[wks_].rearrange("(k p) o -> p k o", p=128)[:, :, mo * 128:(mo + MC) * 128]))
        for nb in range(NB):
            proj_fm(p, wA, lnT, qa, MC * 128, bias_cols=bc, pbanks=pb[0:4],
                    name=uniq + f"_p{tag}a{nb}", tblocks=[nb], out_off=nb * 512)
            proj_fm(p, wB, lnT, qb, MC * 128, bias_cols=bcs, pbanks=pb[0:4],
                    name=uniq + f"_p{tag}b{nb}", tblocks=[nb], out_off=nb * 512)
            csl = slice((nb % 4) * 512, (nb % 4) * 512 + 512)
            for m in range(MC):
                p.stage(uniq + f"_rc{tag}{nb}_{m}")
                p.comp("vector", lambda v, m=m, csl=csl: v.tensor_tensor(
                    tA[:], qa[:, m, :], ccc[:, csl], ALU.mult))
                p.comp("gpsimd", lambda g, m=m, csl=csl: g.tensor_tensor(
                    tB[:], qb[:, m, :], ssg[:, csl], ALU.mult))
                p.stage(uniq + f"_rd{tag}{nb}_{m}")
                p.comp("vector", lambda v, m=m, nb=nb, dst=dst, mo=mo: v.tensor_tensor(
                    dst[:, mo + m, nb * 512:(nb + 1) * 512], tA[:], tB[:], ALU.add))
        f_wB(); f_wA()
    f_tB(); f_tA(); f_qb(); f_qa()
    v_tok, f_vt = sbuf_tmp(p, uniq + "_vtok", [128, NT, 256], BF16)
    wV, f_wV = sbuf_tmp(p, uniq + "_wV", [128, KC, 256], BF16)
    p.stage(uniq + "_wlv")
    p.dma("gpsimd", lambda g: g.dma_start(out=wV[:], in_=w["Wv"].rearrange("(k p) o -> p k o", p=128)))
    for ti in range(NT):
        p.stage(uniq + f"_vm{ti}")
        def vmm(t, ti=ti):
            last = None
            for k in range(KC):
                last = t.matmul(pb[4][:, 0:256], lnT[:, k, ti * 128:(ti + 1) * 128],
                                wV[:, k, :], start=(k == 0), stop=(k == KC - 1))
            return last
        p.comp("tensor", vmm, n=KC)
        p.stage(uniq + f"_ve{ti}")
        p.comp("vector", lambda v, ti=ti: v.tensor_tensor(v_tok[:, ti, :], pb[4][:, 0:256],
                                                          bvb[:, 0:256], ALU.add))
    f_wV()
    f_lnT()
    if PH <= 1:
        for ff in (f_vt, f_krop, f_qrop, f_ssg, f_ccc):
            ff()
        for ff in (f_ns2, f_ns1, f_bvb, f_bksc, f_bkc, f_bqsc, f_bqc,
                   f_pos, f_invf, f_pmc, f_sg, f_i8, f_srow, f_esk, f_mkN, f_mkP):
            ff()
        return

    ebuf, f_eb = sbuf_tmp(p, uniq + "_ebuf", [128, 3, 2048], BF16)
    etmp, f_et = sbuf_tmp(p, uniq + "_etmp", [128, 512], F32)
    denf, f_df = sbuf_tmp(p, uniq + "_denf", [128, 16], F32)
    rden, f_rd = sbuf_tmp(p, uniq + "_rden", [128, 16], F32)
    otok, f_ot = sbuf_tmp(p, uniq + "_otok", [128, 1024], BF16)
    oblk, f_ob = sbuf_tmp(p, uniq + "_oblk", [128, KC, 128], BF16)
    for b in range(2):
        for qt in range(NTS):
            u3 = uniq + f"_a{b}_{qt}"
            qg = b * NTS + qt
            kts = [kt for kt in (qt - 1, qt, qt + 1) if 0 <= kt < NTS]
            if AT < 1:
                continue
            for ci, kt in enumerate(kts):
                kg = b * NTS + kt
                p.stage(u3 + f"sm{ci}")
                def smms(t, kg=kg, qg=qg):
                    last = None
                    for i in range(16):
                        base = (i % 2) * 64 * B64
                        g = QPERM[i] // 4
                        bk_, col_ = (i % 2) * 2 + (i // 2) // 4, ((i // 2) % 4) * 128
                        last = t.matmul(pb[bk_][:, col_:col_ + 128],
                                        krop[base:base + 64, g // 2, kg * 128:(kg + 1) * 128],
                                        qrop[base:base + 64, i // 2, qg * 128:(qg + 1) * 128],
                                        start=True, stop=True)
                    return last
                p.comp("tensor", smms, n=16)
                if kt == qt:
                    p.stage(u3 + f"se{ci}")
                    def sev(s, ci=ci):
                        last = None
                        for bk in range(4):
                            last = s.activation(ebuf[:, ci, bk * 512:(bk + 1) * 512],
                                                pb[bk][:, 0:512], AF.Exp)
                        return last
                    p.comp("scalar", sev, n=4)
                else:
                    msk = mkP if kt < qt else mkN
                    for bk in range(4):
                        p.stage(u3 + f"se{ci}_{bk}")
                        p.comp("scalar", lambda s, bk=bk: s.activation(etmp[:], pb[bk][:, 0:512], AF.Exp))
                        p.stage(u3 + f"sf{ci}_{bk}")
                        p.comp("vector", lambda v, ci=ci, bk=bk, msk=msk: v.tensor_tensor(
                            ebuf[:, ci, bk * 512:(bk + 1) * 512], etmp[:], msk[:], ALU.mult))
            if AT < 2:
                continue
            p.stage(u3 + "dn")
            def dmm(t, kts=kts):
                last = None
                for i in range(16):
                    for ci in range(len(kts)):
                        eo = ((i % 2) * 2 + (i // 2) // 4) * 512 + ((i // 2) % 4) * 128
                        last = t.matmul(pb[4][:, i * 8:i * 8 + 1], ebuf[:, ci, eo:eo + 128],
                                        c["ones_colb"][:], start=(ci == 0), stop=(ci == len(kts) - 1))
                return last
            p.comp("tensor", dmm, n=48)
            p.stage(u3 + "dv")
            p.comp("vector", lambda v: v.tensor_tensor(denf[:], pb[4][:, 0:128:8], esk[:], ALU.add))
            p.stage(u3 + "dr")
            p.comp("vector", lambda v: v.reciprocal(rden[:], denf[:]))
            if AT < 3:
                continue
            p.stage(u3 + "om")
            def omm(t, kts=kts, b=b):
                last = None
                for i in range(16):
                    pbx = pb[5] if i < 8 else pb[0]
                    off = (i % 8) * 64
                    g = QPERM[i] // 4
                    eo = ((i % 2) * 2 + (i // 2) // 4) * 512 + ((i // 2) % 4) * 128
                    for ci, kt in enumerate(kts):
                        kg = b * NTS + kt
                        last = t.matmul(pbx[:, off:off + 64], ebuf[:, ci, eo:eo + 128],
                                        v_tok[:, kg, g * 64:(g + 1) * 64],
                                        start=(ci == 0), stop=(ci == len(kts) - 1))
                return last
            p.comp("tensor", omm, n=48)
            p.stage(u3 + "oe")
            def oev(v):
                last = None
                for i in range(16):
                    pbx = pb[5] if i < 8 else pb[0]
                    off = (i % 8) * 64
                    last = v.tensor_scalar(otok[:, i * 64:i * 64 + 64], pbx[:, off:off + 64],
                                           rden[:, i:i + 1], None, ALU.mult)
                return last
            p.comp("vector", oev, n=16)
            if AT < 4:
                continue
            p.stage(u3 + "tr")
            def otr(t):
                last = None
                for m in range(KC):
                    last = t.transpose(pbt0[:, m * 128:(m + 1) * 128],
                                       otok[:, m * 128:(m + 1) * 128], c["idb"][:])
                return last
            p.comp("tensor", otr, n=KC)
            p.stage(u3 + "te")
            p.comp("vector", lambda v: v.tensor_copy(oblk[:], pbt0[:].rearrange("p (k f) -> p k f", f=128)))
            p.stage(u3 + "td")
            p.dma("sync", lambda s, qg=qg: s.dma_start(out=oT_d.ap()[:, :, qg * 128:(qg + 1) * 128],
                                                       in_=oblk[:]))
    f_ob(); f_ot(); f_rd(); f_df(); f_et(); f_eb()
    f_vt(); f_krop(); f_qrop(); f_ssg(); f_ccc()
    if PH <= 2:
        for ff in (f_ns2, f_ns1, f_bvb, f_bksc, f_bkc, f_bqsc, f_bqc,
                   f_pos, f_invf, f_pmc, f_sg, f_i8, f_srow, f_esk, f_mkN, f_mkP):
            ff()
        return

    h1T, f_h1T = sbuf_tmp(p, uniq + "_h1T", [128, KC, T], BF16)
    coefsb, f_coef = sbuf_tmp(p, uniq + "_coefsb", [128, NT, 8], F32)
    rstd2c, f_rs2 = sbuf_tmp(p, uniq + "_rs2c", [128, NT], F32)
    bob, f_bob = bias_bcast_row(p, c, uniq + "_bob", w["bo"], 1024, pb[0])
    wo, f_wo = sbuf_tmp(p, uniq + "_wo", [128, KC, D], BF16)
    obuf, f_obuf = sbuf_tmp(p, uniq + "_obuf", [128, KC, 512], BF16)
    hidt4, f_hid = sbuf_tmp(p, uniq + "_hidt4", [128, 4, D], BF16)
    h1buf, f_h1b = sbuf_tmp(p, uniq + "_h1buf", [128, D], F32)
    sqd, f_sqd = sbuf_tmp(p, uniq + "_sqd", [128, D], F32)
    p.stage(uniq + "_wol")
    p.dma("gpsimd", lambda g: g.dma_start(out=wo[:], in_=w["Wo"].rearrange("(k p) o -> p k o", p=128)))
    hsrc = hid_d.ap().rearrange("(n p) d -> p n d", p=128)
    h1dst = h1_d.ap().rearrange("(n p) d -> p n d", p=128)
    for nb in range(NB):
        p.stage(uniq + f"_y0{nb}")
        p.dma("sync", lambda s, nb=nb: s.dma_start(out=obuf[:], in_=oT_d.ap()[:, :, nb * 512:(nb + 1) * 512]))
        p.dma("sync", lambda s, nb=nb: s.dma_start(out=hidt4[:], in_=hsrc[:, nb * 4:(nb + 1) * 4]))
        for jt in range(4):
            ti = nb * 4 + jt
            p.stage(uniq + f"_y1{ti}")
            def ymm(t, jt=jt):
                last = None
                for half in range(2):
                    pbx = pb[4 + half]
                    for k in range(KC):
                        last = t.matmul(pbx[:, 0:512], obuf[:, k, jt * 128:(jt + 1) * 128],
                                        wo[:, k, half * 512:(half + 1) * 512],
                                        start=(k == 0), stop=(k == KC - 1))
                return last
            p.comp("tensor", ymm, n=2 * KC)
            p.stage(uniq + f"_y2{ti}")
            def yev(v, jt=jt):
                last = None
                for half in range(2):
                    last = v.tensor_tensor(sqd[:, half * 512:(half + 1) * 512], pb[4 + half][:, 0:512],
                                           bob[:, half * 512:(half + 1) * 512], ALU.add)
                return last
            p.comp("vector", yev, n=2)
            p.stage(uniq + f"_y3{ti}")
            def yad(v, jt=jt):
                return v.tensor_tensor(h1buf[:], sqd[:], hidt4[:, jt, :], ALU.add)
            p.comp("vector", yad)
            p.stage(uniq + f"_y4{ti}")
            p.comp("scalar", lambda s, ti=ti: s.activation(sqd[:], h1buf[:], AF.Square,
                                                           accum_out=rstd2c[:, ti:ti + 1]))
            p.dma("sync", lambda s, ti=ti: s.dma_start(out=h1dst[:, ti], in_=h1buf[:]))
            p.stage(uniq + f"_y5{ti}")
            def htr(t):
                last = None
                for m in range(KC):
                    pbx = pb[m // 4]
                    last = t.transpose(pbx[:, (m % 4) * 128:(m % 4) * 128 + 128],
                                       h1buf[:, m * 128:(m + 1) * 128], c["idf"][:])
                return last
            p.comp("tensor", htr, n=KC)
            p.stage(uniq + f"_y6{ti}")
            def hte(v, ti=ti):
                v.tensor_copy(h1T[:, 0:4, ti * 128:(ti + 1) * 128],
                              pb[0][:].rearrange("p (k f) -> p k f", f=128))
                return v.tensor_copy(h1T[:, 4:8, ti * 128:(ti + 1) * 128],
                                     pb[1][:].rearrange("p (k f) -> p k f", f=128))
            p.comp("vector", hte, n=2)
    f_sqd(); f_h1b(); f_hid(); f_obuf(); f_wo(); f_bob()
    if PH <= 3:
        for ff in (f_rs2, f_coef, f_h1T):
            ff()
        for ff in (f_ns2, f_ns1, f_bvb, f_bksc, f_bkc, f_bqsc, f_bqc,
                   f_pos, f_invf, f_pmc, f_sg, f_i8, f_srow, f_esk, f_mkN, f_mkP):
            ff()
        return

    p.stage(uniq + "_n0")
    p.comp("vector", lambda v: v.tensor_scalar(rstd2c[:], rstd2c[:], 1.0 / D, 1e-5, ALU.mult, ALU.add))
    p.stage(uniq + "_n1")
    p.comp("scalar", lambda s: s.activation(rstd2c[:], rstd2c[:], AF.Sqrt))
    p.stage(uniq + "_n2")
    p.comp("vector", lambda v: v.reciprocal(rstd2c[:], rstd2c[:]))
    p.stage(uniq + "_n2b")
    p.comp("tensor", lambda t: t.transpose(pb[3][0:NT, 0:128], rstd2c[:], c["idf"][:]))
    rr32, f_rr32 = sbuf_tmp(p, uniq + "_rr32", [NT, 128], F32)
    p.stage(uniq + "_n2c")
    p.comp("vector", lambda v: v.tensor_copy(rr32[:], pb[3][0:NT, 0:128]))
    p.stage(uniq + "_n3")
    p.dma("sync", lambda s: s.dma_start(out=rsd2_d.ap(), in_=rr32[:]))
    f_rr32()
    f_rs2()
    rs2row, f_rs2r = sbuf_tmp(p, uniq + "_rs2row", [1, T], F32)
    p.stage(uniq + "_n4")
    p.dma("sync", lambda s: s.dma_start(out=rs2row[:],
                                        in_=rsd2_d.ap().rearrange("n p -> (n p)").rearrange("(a d) -> a d", a=1)))
    p.stage(uniq + "_n5")
    def ns2ap(v):
        last = None
        for k in range(KC):
            last = v.tensor_scalar(h1T[:, k, :], h1T[:, k, :], ns2c[:, k:k + 1], None, ALU.mult)
        return last
    p.comp("vector", ns2ap, n=KC)
    for nb in range(NB):
        p.stage(uniq + f"_n6{nb}")
        p.comp("tensor", lambda t, nb=nb: t.matmul(pb[2][:, 0:512], c["ones_rowf"][0:1, :],
                                                   rs2row[0:1, nb * 512:(nb + 1) * 512],
                                                   start=True, stop=True))
        p.stage(uniq + f"_n7{nb}")
        def rap2(v, nb=nb):
            last = None
            for k in range(KC):
                last = v.tensor_tensor(h1T[:, k, nb * 512:(nb + 1) * 512],
                                       h1T[:, k, nb * 512:(nb + 1) * 512], pb[2][:, 0:512], ALU.mult)
            return last
        p.comp("vector", rap2, n=KC)
    f_rs2r()

    rbrow, f_rbr = sbuf_tmp(p, uniq + "_rbrow", [1, 8], BF16, side="right")
    rwsb, f_rw = sbuf_tmp(p, uniq + "_rwsb", [128, KC, 8], BF16)
    coefT, f_coefT = sbuf_tmp(p, uniq + "_coefT", [8, T], F32)
    lg, f_lg = sbuf_tmp(p, uniq + "_lg", [128, 8], F32)
    srt, f_srt = sbuf_tmp(p, uniq + "_srt", [128, 8], F32)
    idxu, f_idxu = sbuf_tmp(p, uniq + "_idxu", [128, 8], U32)
    idxf, f_idxf = sbuf_tmp(p, uniq + "_idxf", [128, 8], F32)
    e4, f_e4 = sbuf_tmp(p, uniq + "_e4", [128, 4], F32)
    nm, f_nm = sbuf_tmp(p, uniq + "_nm", [128, 2], F32)
    wtop, f_wtop = sbuf_tmp(p, uniq + "_wtop", [128, 4], F32)
    mk8, f_mk8 = sbuf_tmp(p, uniq + "_mk8", [128, 8], F32)
    p.stage(uniq + "_rl")
    p.dma("gpsimd", lambda g: g.dma_start(out=rbrow[:], in_=w["router_b"].rearrange("(a d) -> a d", a=1)))
    p.dma("gpsimd", lambda g: g.dma_start(out=rwsb[:], in_=w["router_W"].rearrange("(k p) o -> p k o", p=128)))
    for ti in range(NT):
        u4 = uniq + f"_g{ti}"
        p.stage(u4 + "m")
        def gmm(t, ti=ti):
            for k in range(KC):
                t.matmul(pb[3][:, 0:8], h1T[:, k, ti * 128:(ti + 1) * 128], rwsb[:, k, :],
                         start=(k == 0), stop=False)
            return t.matmul(pb[3][:, 0:8], c["ones_rowb"][0:1, 0:128], rbrow[:],
                            start=False, stop=True)
        p.comp("tensor", gmm, n=KC + 1)
        p.stage(u4 + "c")
        p.comp("vector", lambda v: v.tensor_copy(lg[:], pb[3][:, 0:8]))
        p.stage(u4 + "x")
        p.comp("vector", lambda v: v.max(srt[:], lg[:]))
        p.stage(u4 + "i")
        p.comp("vector", lambda v: v.max_index(idxu[:], srt[:], lg[:]))
        p.stage(u4 + "if")
        p.comp("vector", lambda v: v.tensor_copy(idxf[:], idxu[:]))
        p.comp("gpsimd", lambda g: g.tensor_scalar(nm[:, 0:1], srt[:, 0:1], -1.0, None, ALU.mult))
        p.stage(u4 + "e")
        p.comp("scalar", lambda s: s.activation(e4[:], srt[:, 0:4], AF.Exp, bias=nm[:, 0:1]))
        p.stage(u4 + "s")
        p.comp("vector", lambda v: v.reduce_sum(nm[:, 1:2], e4[:], axis=AX.X))
        p.stage(u4 + "s2")
        p.comp("vector", lambda v: v.tensor_scalar(nm[:, 1:2], nm[:, 1:2], 4.0, None, ALU.mult))
        p.stage(u4 + "r")
        p.comp("vector", lambda v: v.reciprocal(nm[:, 1:2], nm[:, 1:2]))
        p.stage(u4 + "w")
        p.comp("vector", lambda v: v.tensor_scalar(wtop[:], e4[:], nm[:, 1:2], None, ALU.mult))
        for k in range(4):
            p.stage(u4 + f"k{k}")
            p.comp("vector", lambda v, k=k: v.tensor_scalar(mk8[:], iota8t[:], idxf[:, k:k + 1],
                                                            None, ALU.is_equal))
            p.stage(u4 + f"k2{k}")
            if k == 0:
                p.comp("vector", lambda v, ti=ti: v.tensor_scalar(
                    coefsb[:, ti, :], mk8[:], wtop[:, 0:1], None, ALU.mult))
            else:
                p.comp("vector", lambda v, k=k: v.tensor_scalar(mk8[:], mk8[:], wtop[:, k:k + 1],
                                                                None, ALU.mult))
                p.stage(u4 + f"k3{k}")
                p.comp("vector", lambda v, ti=ti: v.tensor_tensor(coefsb[:, ti, :], coefsb[:, ti, :],
                                                                  mk8[:], ALU.add))
        p.stage(u4 + "t1")
        p.comp("tensor", lambda t, ti=ti: t.transpose(pb[5][0:8, 0:128], coefsb[:, ti, :], c["idf"][:]))
        p.stage(u4 + "t2")
        p.comp("vector", lambda v, ti=ti: v.tensor_copy(coefT[:, ti * 128:(ti + 1) * 128],
                                                        pb[5][0:8, 0:128]))
    p.stage(uniq + "_cd")
    p.dma("sync", lambda s: s.dma_start(out=coef_d.ap(), in_=coefT[:]))
    for ff in (f_mk8, f_wtop, f_nm, f_e4, f_idxf, f_idxu, f_srt, f_lg, f_coefT, f_rw):
        ff()
    f_rbr()
    f_coef()
    if PH <= 4:
        f_h1T()
        for ff in (f_ns2, f_ns1, f_bvb, f_bksc, f_bkc, f_bqsc, f_bqc,
                   f_pos, f_invf, f_pmc, f_sg, f_i8, f_srow, f_esk, f_mkN, f_mkP):
            ff()
        return

    guT, f_gu = sbuf_tmp(p, uniq + "_guT", [128, 16, 512], BF16)
    sS, f_sS = sbuf_tmp(p, uniq + "_sS", [128, 8, 512], BF16)
    coefr, f_cr = sbuf_tmp(p, uniq + "_coefr", [1, 512], F32)
    for e in range(8):
        u5 = uniq + f"_e{e}"
        wge, f_wge = sbuf_tmp(p, uniq + f"_wge{e}", [128, KC, 2048], BF16)
        b1c, f_b1c = load_bias_cols(p, uniq + f"_b1c{e}", w["gup_b"][e], 2048)
        p.stage(u5 + "l")
        p.dma("gpsimd", lambda g, wge=wge, e=e: g.dma_start(
            out=wge[:], in_=w["gup"][e].rearrange("(k p) o -> p k o", p=128)))
        for nb in range(NB):
            proj_fm(p, wge, h1T, guT, 2048, bias_cols=b1c, pbanks=pb[0:4],
                    name=u5 + f"g{nb}", tblocks=[nb], out_off=nb * 512)
            p.stage(u5 + f"cl{nb}")
            p.dma("sync", lambda s, e=e, nb=nb: s.dma_start(
                out=coefr[:],
                in_=coef_d.ap()[e][nb * 512:(nb + 1) * 512].rearrange("(a d) -> a d", a=1)))
            p.stage(u5 + f"cb{nb}")
            p.comp("tensor", lambda t, nb=nb: t.matmul(
                pb[4][:, 0:512], c["ones_rowf"][0:1, :], coefr[0:1, :],
                start=True, stop=True))
            def gcl(v):
                last = None
                for hh in range(KC):
                    last = v.tensor_scalar(guT[:, hh, :], guT[:, hh, :], 7.0, None, ALU.min)
                return last
            p.comp("vector", gcl, n=KC)
            p.stage(u5 + f"g2{nb}")
            def gsi(s):
                last = None
                for hh in range(KC):
                    last = s.activation(sS[:, hh, :], guT[:, hh, :], AF.Silu, scale=1.702)
                return last
            p.comp("scalar", gsi, n=KC)
            def ucl(v):
                last = None
                for hh in range(KC):
                    last = v.tensor_scalar(guT[:, hh + 8, :], guT[:, hh + 8, :], 7.0, -7.0,
                                           ALU.min, ALU.max)
                return last
            p.comp("vector", ucl, n=KC)
            p.stage(u5 + f"g3{nb}")
            def gmu(v):
                last = None
                for hh in range(KC):
                    last = v.scalar_tensor_tensor(guT[:, hh, :], guT[:, hh + 8, :], 1.0,
                                                  sS[:, hh, :], ALU.add, ALU.mult)
                return last
            p.comp("vector", gmu, n=KC)
            p.stage(u5 + f"g4{nb}")
            def gsc(v):
                last = None
                for hh in range(KC):
                    last = v.tensor_tensor(guT[:, hh, :], guT[:, hh, :], pb[4][:, 0:512], ALU.mult)
                return last
            p.comp("vector", gsc, n=KC)
            p.stage(u5 + f"g5{nb}")
            p.dma("sync", lambda s, e=e, nb=nb: s.dma_start(
                out=gated_d.ap()[e][:, :, nb * 512:(nb + 1) * 512], in_=guT[:, 0:8, :]))
        f_b1c()
        f_wge()
    f_cr(); f_sS(); f_gu()
    f_h1T()
    if PH <= 5:
        for ff in (f_ns2, f_ns1, f_bvb, f_bksc, f_bkc, f_bqsc, f_bqc,
                   f_pos, f_invf, f_pmc, f_sg, f_i8, f_srow, f_esk, f_mkN, f_mkP):
            ff()
        return

    dbrows, f_dbr = sbuf_tmp(p, uniq + "_dbrows", [1, 8192], BF16, side="right")
    wde8, f_wd8 = sbuf_tmp(p, uniq + "_wde8", [128, 64, 1024], BF16)
    gt8, f_gt8 = sbuf_tmp(p, uniq + "_gt8", [128, 8, KC, 128], BF16)
    crow, f_crow = sbuf_tmp(p, uniq + "_crow", [1, 8, 128], BF16)
    h1t, f_h1t = sbuf_tmp(p, uniq + "_h1t", [128, D], F32)
    h2b, f_h2b = sbuf_tmp(p, uniq + "_h2b", [128, D], F32)
    p.stage(uniq + "_dl")
    p.dma("gpsimd", lambda g: g.dma_start(out=dbrows[:], in_=w["down_b"].rearrange("e h -> (e h)").rearrange("(a d) -> a d", a=1)))
    for e in range(8):
        p.dma("gpsimd", lambda g, e=e: g.dma_start(
            out=wde8[:, e * 8:(e + 1) * 8, :],
            in_=w["down"][e].rearrange("(k p) o -> p k o", p=128)))
    p.stage(uniq + "_ds")
    def wds(v):
        last = None
        for e in range(8):
            for k in range(KC):
                last = v.tensor_scalar(wde8[:, e * 8 + k, :], wde8[:, e * 8 + k, :],
                                       1.0 / 1.702, None, ALU.mult)
        return last
    p.comp("vector", wds, n=64)
    h2dst = h2_d.ap().rearrange("(n p) d -> p n d", p=128)
    for ti in range(NT):
        u6 = uniq + f"_d{ti}"
        p.stage(u6 + "l")
        for e in range(8):
            p.dma("sync", lambda s, ti=ti, e=e: s.dma_start(
                out=gt8[:, e], in_=gated_d.ap()[e][:, :, ti * 128:(ti + 1) * 128]))
            p.dma("gpsimd", lambda g, ti=ti, e=e: g.dma_start(
                out=crow[:, e, :],
                in_=coef_d.ap()[e][ti * 128:(ti + 1) * 128].rearrange("(a d) -> a d", a=1)))
        p.dma("sync", lambda s, ti=ti: s.dma_start(out=h1t[:], in_=h1dst[:, ti]))
        p.stage(u6 + "m")
        def dmms(t):
            last = None
            for half in range(2):
                pbx = pb[4 + half]
                first = True
                for e in range(8):
                    for k in range(KC):
                        t.matmul(pbx[:, 0:512], gt8[:, e, k, :],
                                 wde8[:, e * 8 + k, half * 512:(half + 1) * 512],
                                 start=first, stop=False)
                        first = False
                    last = t.matmul(pbx[:, 0:512], crow[0:1, e, :],
                                    dbrows[0:1, e * 1024 + half * 512:e * 1024 + half * 512 + 512],
                                    start=False, stop=(e == 7))
            return last
        p.comp("tensor", dmms, n=144)
        p.stage(u6 + "e")
        def dev(v):
            last = None
            for half in range(2):
                last = v.tensor_tensor(h2b[:, half * 512:(half + 1) * 512], pb[4 + half][:, 0:512],
                                       h1t[:, half * 512:(half + 1) * 512], ALU.add)
            return last
        p.comp("vector", dev, n=2)
        p.stage(u6 + "w")
        p.dma("sync", lambda s, ti=ti: s.dma_start(out=h2dst[:, ti], in_=h2b[:]))
    f_h2b(); f_h1t(); f_crow(); f_gt8(); f_wd8()
    f_dbr()

    epilogue_ln_add(p, c, h2_d.ap(), xin_ap, xout_ap, uniq=uniq)

    for ff in (f_ns2, f_ns1, f_bvb, f_bksc, f_bkc, f_bqsc, f_bqc,
               f_pos, f_invf, f_pmc, f_sg, f_i8, f_srow, f_esk, f_mkN, f_mkP):
        ff()


WSPEC_L3A = [
    ("Wq", "L3_attn_Wq", lambda a: _wq(a, False)),
    ("Wqs", "L3_attn_Wq", lambda a: _wq(a, True)),
    ("bq", "L3_attn_bq", lambda a: _bq(a, False)),
    ("bqs", "L3_attn_bq", lambda a: _bq(a, True)),
    ("Wk", "L3_attn_Wk", lambda a: _wk(a, False)),
    ("Wks", "L3_attn_Wk", lambda a: _wk(a, True)),
    ("bk", "L3_attn_bk", lambda a: _bk(a, False)),
    ("bks", "L3_attn_bk", lambda a: _bk(a, True)),
    ("Wv", "L3_attn_Wv", lambda a: a.reshape(1024, 256)),
    ("bv", "L3_attn_bv", lambda a: a.reshape(256)),
    ("Wo", "L3_attn_Wo", _wo),
    ("bo", "L3_attn_bo", None),
    ("sinks", "L3_attn_sinks", lambda a: a[np.array(QPERM)]),
    ("ns1", "L3_attn_attn_norm_scale", None),
    ("ns2", "L3_attn_moe_norm_scale", None),
    ("router_W", "L3_attn_router_W", None),
    ("router_b", "L3_attn_router_b", None),
    ("gup", "L3_attn_gate_up_proj", None),
    ("gup_b", "L3_attn_gate_up_bias", None),
    ("down", "L3_attn_down_proj", None),
    ("down_b", "L3_attn_down_bias", None),
]


_doc_L3F = """L3_ffn: h = gelu(bitlin(ln, w1, b1)); y = bitlin2(h, w2, b2).
bitlin(x,w,b) = act_quant(rmsnorm(x)) @ wq.T + b; wq = tern/s_inv with global scale.
Effective: y = s0*(xq @ tern.T) + b where s0 = clip(mean|w|,1e-5)."""
import numpy as np



HID = 4096


def emit_rms_quant(p, c, lt_ap_fn, G, sq, qi, st, u):
    for j in range(G):
        p.stage(u + f"s{j}")
        p.comp("scalar", lambda s, j=j: s.activation(sq[:], lt_ap_fn(j), AF.Square, accum_out=st[:, 0:1]))
        p.comp("vector", lambda v, j=j: v.reduce_max(st[:, 1:2], lt_ap_fn(j), axis=AX.X,
                                                     apply_absolute_value=True))
        p.stage(u + f"m{j}")
        p.comp("vector", lambda v, j=j: v.tensor_single_scalar(
            st[:, 2:3], st[:, 0:1], 1.0 / lt_ap_fn(j).free_size(), ALU.mult))
        p.stage(u + f"sd{j}")
        p.comp("scalar", lambda s: s.activation(st[:, 2:3], st[:, 2:3], AF.Sqrt,
                                                bias=c["eps_ln"][:, 0:1]))
        p.stage(u + f"rc{j}")
        p.comp("vector", lambda v: v.reciprocal(st[:, 2:3], st[:, 2:3]))
        p.stage(u + f"c1{j}")
        p.comp("vector", lambda v: v.tensor_tensor(st[:, 1:2], st[:, 1:2], st[:, 2:3], ALU.mult))
        p.stage(u + f"c2{j}")
        def c2(v):
            v.tensor_single_scalar(st[:, 1:2], st[:, 1:2], 1e-5, ALU.max)
            return v.tensor_single_scalar(st[:, 4:5], st[:, 1:2], 1.0 / 127.0, ALU.mult)
        p.comp("vector", c2, n=2)
        p.stage(u + f"c3{j}")
        p.comp("vector", lambda v: v.reciprocal(st[:, 3:4], st[:, 4:5]))
        p.stage(u + f"c4{j}")
        p.comp("vector", lambda v: v.tensor_tensor(st[:, 3:4], st[:, 3:4], st[:, 2:3], ALU.mult))
        p.stage(u + f"a1{j}")
        p.comp("vector", lambda v, j=j: v.tensor_scalar(sq[:], lt_ap_fn(j), st[:, 3:4], None, ALU.mult))
        half = lt_ap_fn(j).free_size() // 2
        for hh in range(2):
            p.stage(u + f"a2{j}_{hh}")
            p.comp("vector", lambda v, hh=hh: v.tensor_copy(
                qi[:, 0:half], sq[:, hh * half:(hh + 1) * half]))
            p.stage(u + f"a3{j}_{hh}")
            p.comp("vector", lambda v, hh=hh: v.tensor_copy(
                sq[:, hh * half:(hh + 1) * half], qi[:, 0:half]))
        p.stage(u + f"a4{j}")
        p.comp("vector", lambda v, j=j: v.tensor_scalar(lt_ap_fn(j), sq[:], st[:, 4:5], None, ALU.mult))


def prep_tern_global(p, c, w_ap, din, dout, tern, scol, rcol, pb, uniq, dblk=512):
    KN = din // 128
    ND = dout // dblk
    wf32, f_w = sbuf_tmp(p, uniq + "_pw", [128, KN, dblk], F32)
    aw, f_aw = sbuf_tmp(p, uniq + "_paw", [128, dblk], F32)
    t1, f_t1 = sbuf_tmp(p, uniq + "_pt1", [128, KN, dblk], BF16)
    t2, f_t2 = sbuf_tmp(p, uniq + "_pt2", [128, KN, dblk], BF16)
    srow, f_sr = sbuf_tmp(p, uniq + "_psr", [1, dout], F32)
    for db in range(ND):
        p.stage(uniq + f"_l{db}")
        p.dma("sync", lambda s, db=db: s.dma_start(
            out=wf32[:], in_=w_ap[:, db * dblk:(db + 1) * dblk].rearrange("(k p) o -> p k o", p=128)))
        for k in range(KN):
            p.stage(uniq + f"_ab{db}_{k}")
            p.comp("scalar", lambda s, k=k: s.activation(aw[:], wf32[:, k, :], AF.Abs))
            p.stage(uniq + f"_mm{db}_{k}")
            p.comp("tensor", lambda t, k=k: t.matmul(pb[0][0:1, 0:dblk], c["ones_colf"][:], aw[:],
                                                     start=(k == 0), stop=(k == KN - 1)))
        p.stage(uniq + f"_ev{db}")
        p.comp("vector", lambda v, db=db: v.tensor_copy(srow[0:1, db * dblk:(db + 1) * dblk],
                                                        pb[0][0:1, 0:dblk]))
    p.stage(uniq + "_tot")
    p.comp("vector", lambda v: v.reduce_sum(srow[0:1, 0:1], srow[:], axis=AX.X))
    p.stage(uniq + "_mean")
    p.comp("vector", lambda v: v.tensor_scalar(srow[0:1, 0:1], srow[0:1, 0:1],
                                               1.0 / (din * dout), 1e-5, ALU.mult, ALU.max))
    p.stage(uniq + "_bc")
    p.comp("tensor", lambda t: t.matmul(pb[1][:, 0:1], c["ones_rowf"][0:1, 0:128],
                                        srow[0:1, 0:1], start=True, stop=True))
    p.stage(uniq + "_bce")
    p.comp("vector", lambda v: v.tensor_copy(scol[:], pb[1][:, 0:1]))
    p.stage(uniq + "_brc")
    p.comp("vector", lambda v: v.reciprocal(rcol[:], scol[:]))
    for db in range(ND):
        p.stage(uniq + f"_l2{db}")
        p.dma("sync", lambda s, db=db: s.dma_start(
            out=wf32[:], in_=w_ap[:, db * dblk:(db + 1) * dblk].rearrange("(k p) o -> p k o", p=128)))
        p.stage(uniq + f"_a2{db}")
        def pa(v, db=db):
            last = None
            for k in range(KN):
                last = v.tensor_scalar(wf32[:, k, :], wf32[:, k, :], rcol[:, 0:1], None, ALU.mult)
            return last
        p.comp("vector", pa, n=KN)
        p.stage(uniq + f"_th{db}")
        def pth(v, db=db):
            last = None
            for k in range(KN):
                last = v.tensor_single_scalar(t1[:, k, :], wf32[:, k, :], 0.5, ALU.is_ge)
            return last
        p.comp("vector", pth, n=KN)
        def pth2(g, db=db):
            last = None
            for k in range(KN):
                last = g.tensor_single_scalar(t2[:, k, :], wf32[:, k, :], -0.5, ALU.is_le)
            return last
        p.comp("gpsimd", pth2, n=KN)
        p.stage(uniq + f"_td{db}")
        def ptd(v, db=db):
            last = None
            for k in range(KN):
                last = v.tensor_tensor(tern[:, k, db * dblk:(db + 1) * dblk],
                                       t1[:, k, :], t2[:, k, :], ALU.subtract)
            return last
        p.comp("vector", ptd, n=KN)
    f_sr(); f_t2(); f_t1(); f_aw(); f_w()


def build_l3_ffn(p, c, xin_ap, xout_ap, w, pb, pbt0, pbt1, uniq="l3f"):
    y_d = p.dram(uniq + "_y", [T, D], F32)
    hq_d = p.dram(uniq + "_hq", [128, 32, T], BF16)

    lnT, f_lnT = sbuf_tmp(p, uniq + "_lnT", [128, KC, T], BF16)
    qst, f_qst = sbuf_tmp(p, uniq + "_qst", [128, 8], F32, side="right")
    sq0, f_sq0 = sbuf_tmp(p, uniq + "_sq0", [128, D], F32)
    qi0, f_qi0 = sbuf_tmp(p, uniq + "_qi0", [128, D // 2], I32)

    def hook(pp, lt, g, G):
        emit_rms_quant(pp, c, lambda j: lt[:, j], G, sq0, qi0, qst, uniq + f"_q{g}_")

    prologue_ln(p, c, xin_ap, lnT, pbt0, pbt1, uniq=uniq, post_hook=hook)
    f_qi0(); f_sq0()

    tern1, f_t1 = sbuf_tmp(p, uniq + "_tern1", [128, KC, HID], BF16)
    s1col, f_s1 = sbuf_tmp(p, uniq + "_s1c", [128, 1], F32, side="right")
    r1col, f_r1 = sbuf_tmp(p, uniq + "_r1c", [128, 1], F32, side="right")
    prep_tern_global(p, c, w["w1"], D, HID, tern1, s1col, r1col, pb, uniq + "1")
    b1b, f_b1b = bias_bcast_row(p, c, uniq + "_b1b", w["b1"], HID, pb[0], dtype=BF16)

    sq, f_sq = sbuf_tmp(p, uniq + "_sq", [128, HID], F32)
    qi, f_qi = sbuf_tmp(p, uniq + "_qi", [128, HID // 2], I32)
    htok, f_ht = sbuf_tmp(p, uniq + "_htok", [128, HID], F32)
    hq, f_hq = sbuf_tmp(p, uniq + "_hqt", [128, HID], BF16)
    trbuf, f_tb = sbuf_tmp(p, uniq + "_trb", [128, 2, 1024], BF16)
    for ti in range(NT):
        for dc in range(HID // 512):
            pbx = pb[dc % 2]
            p.stage(uniq + f"_h{ti}_{dc}")
            def hmm(t, ti=ti, dc=dc, pbx=pbx):
                last = None
                for k in range(KC):
                    last = t.matmul(pbx[:, 0:512], lnT[:, k, ti * 128:(ti + 1) * 128],
                                    tern1[:, k, dc * 512:(dc + 1) * 512],
                                    start=(k == 0), stop=(k == KC - 1))
                return last
            p.comp("tensor", hmm, n=KC)
            p.stage(uniq + f"_hs{ti}_{dc}")
            p.comp("vector", lambda v, ti=ti, dc=dc, pbx=pbx: v.scalar_tensor_tensor(
                htok[:, dc * 512:(dc + 1) * 512], pbx[:, 0:512], s1col[:, 0:1],
                b1b[:, dc * 512:(dc + 1) * 512], ALU.mult, ALU.add))
        p.stage(uniq + f"_hg{ti}")
        p.comp("scalar", lambda s, ti=ti: s.activation(htok[:], htok[:], AF.Gelu))
        emit_rms_quant(p, c, lambda j: htok[:], 1, sq, qi, qst, uniq + f"_hq{ti}_")
        p.stage(uniq + f"_hcopy{ti}")
        p.comp("vector", lambda v: v.tensor_copy(hq[:], htok[:]))
        for rr in range(2):
            p.stage(uniq + f"_htr{ti}_{rr}")
            for half, pbt in ((0, pbt0), (1, pbt1)):
                def trs(t, rr=rr, half=half, pbt=pbt):
                    last = None
                    for k in range(8):
                        ch = rr * 16 + half * 8 + k
                        last = t.transpose(pbt[:, k * 128:(k + 1) * 128],
                                           hq[:, ch * 128:(ch + 1) * 128], c["idb"][:])
                    return last
                p.comp("tensor", trs, n=8)
            p.stage(uniq + f"_hte{ti}_{rr}")
            p.comp("vector", lambda v: v.tensor_copy(trbuf[:, 0, :], pbt0[:]))
            p.comp("scalar", lambda s: s.copy(trbuf[:, 1, :], pbt1[:]))
            p.stage(uniq + f"_htst{ti}_{rr}")
            p.dma("sync", lambda s, ti=ti, rr=rr: s.dma_start(
                out=hq_d.ap()[:, rr * 16:(rr + 1) * 16, ti * 128:(ti + 1) * 128],
                in_=trbuf[:].rearrange("p g (k t) -> p (g k) t", t=128)))
    f_tb(); f_hq(); f_ht(); f_qi(); f_sq()
    f_b1b(); f_t1()
    f_lnT()

    tern2, f_t2 = sbuf_tmp(p, uniq + "_tern2", [128, 32, D], BF16)
    s2col, f_s2 = sbuf_tmp(p, uniq + "_s2c", [128, 1], F32, side="right")
    r2col, f_r2 = sbuf_tmp(p, uniq + "_r2c", [128, 1], F32, side="right")
    prep_tern_global(p, c, w["w2"], HID, D, tern2, s2col, r2col, pb, uniq + "2", dblk=256)
    b2b, f_b2b = bias_bcast_row(p, c, uniq + "_b2b", w["b2"], D, pb[0])

    hA, f_hA = sbuf_tmp(p, uniq + "_hA", [128, 32, 512], BF16)
    hB, f_hB = sbuf_tmp(p, uniq + "_hB", [128, 32, 512], BF16)
    yo, f_yo = sbuf_tmp(p, uniq + "_yo", [128, 4, D], F32)
    ydst = y_d.ap().rearrange("(n p) d -> p n d", p=128)
    p.stage(uniq + "_yld0")
    p.dma("sync", lambda s: s.dma_start(out=hA[:], in_=hq_d.ap()[:, :, 0:512]))
    for nb in range(NB):
        hsrc = hA if nb % 2 == 0 else hB
        p.stage(uniq + f"_ypre{nb}")
        if nb + 1 < NB:
            nxt = hB if nb % 2 == 0 else hA
            p.dma("sync", lambda s, nb=nb, nxt=nxt: s.dma_start(
                out=nxt[:], in_=hq_d.ap()[:, :, (nb + 1) * 512:(nb + 2) * 512]))
        for jt in range(4):
            p.stage(uniq + f"_ymm{nb}_{jt}")
            def ymm(t, jt=jt, hsrc=hsrc):
                last = None
                for half in range(2):
                    pbx = pb[4 + half]
                    for k in range(32):
                        last = t.matmul(pbx[:, 0:512], hsrc[:, k, jt * 128:(jt + 1) * 128],
                                        tern2[:, k, half * 512:(half + 1) * 512],
                                        start=(k == 0), stop=(k == 31))
                return last
            p.comp("tensor", ymm, n=64)
            p.stage(uniq + f"_yev{nb}_{jt}")
            def yev(v, jt=jt):
                v.scalar_tensor_tensor(yo[:, jt, 0:512], pb[4][:, 0:512], s2col[:, 0:1],
                                       b2b[:, 0:512], ALU.mult, ALU.add)
                return v.scalar_tensor_tensor(yo[:, jt, 512:1024], pb[5][:, 0:512], s2col[:, 0:1],
                                              b2b[:, 512:1024], ALU.mult, ALU.add)
            p.comp("vector", yev, n=2)
        p.stage(uniq + f"_yst{nb}")
        p.dma("sync", lambda s, nb=nb: s.dma_start(out=ydst[:, nb * 4:(nb + 1) * 4], in_=yo[:]))
    f_yo(); f_hB(); f_hA(); f_b2b(); f_t2()
    f_r2(); f_s2(); f_r1(); f_s1()
    f_qst()

    epilogue_ln_add(p, c, y_d.ap(), xin_ap, xout_ap, uniq=uniq)


WSPEC_L3F = [("w1", "L3_ffn_w1", "T"), ("b1", "L3_ffn_b1", None),
         ("w2", "L3_ffn_w2", "T"), ("b2", "L3_ffn_b2", None)]


import inspect as _inspect
from concourse.bass_utils import run_bass_kernel_spmd as _run_spmd

_BLOCKS = [
    ("b0", build_l0_attn, WSPEC_L0A, False),
    ("b1", build_l0_ffn, WSPEC_L0F, True),
    ("b2", build_l1_attn, WSPEC_L1A, False),
    ("b3", build_l1_ffn, WSPEC_L1F, False),
    ("b4", build_l2_attn, WSPEC_L2A, False),
    ("b5", build_l2_ffn, WSPEC_L2F, False),
    ("b6", build_l3_attn, WSPEC_L3A, False),
    ("b7", build_l3_ffn, WSPEC_L3F, False),
]


def _build_block(builder, wshapes, needs_rank):
    import concourse.bass as bass
    nc = bass.Bass("TRN2", target_bir_lowering=False, detect_race_conditions=False)
    p = Prog(nc)
    xin = p.dram("xin", [T, D], F32, kind="ExternalInput")
    xout = p.dram("xout", [T, D], F32, kind="ExternalOutput")
    w = {}
    for kk, sh in wshapes.items():
        w[kk] = p.dram("w_" + kk, list(sh), F32, kind="ExternalInput").ap()
    pb = [p.psum(f"pb{i}", [128, 512], F32) for i in range(6)]
    pbt0 = p.psum("pbt0", [128, 1024], BF16)
    pbt1 = p.psum("pbt1", [128, 1024], BF16)
    c = build_consts(p)
    if needs_rank:
        rk = p.dram("rankmask", [8], F32, kind="ExternalInput")
        builder(p, c, xin.ap(), xout.ap(), w, pb, pbt0, pbt1, rankmask_ap=rk.ap())
    else:
        builder(p, c, xin.ap(), xout.ap(), w, pb, pbt0, pbt1)
    return p.emit()


def kernel(**inputs):
    x = np.asarray(inputs["L0_attn_x"], dtype=np.float32)
    B_, N_, D_ = x.shape
    n_cores = 8
    per = B_ // n_cores
    cur = [np.ascontiguousarray(x[per * ci:per * ci + per].reshape(T, D)) for ci in range(n_cores)]
    rmasks = [(np.arange(8) < ci).astype(np.float32) for ci in range(n_cores)]
    for (tag, builder, wspec, needs_rank) in _BLOCKS:
        wmap, wshapes = {}, {}
        for kk, name, tr in wspec:
            a = np.asarray(inputs[name], dtype=np.float32)
            if tr == "T":
                a = np.ascontiguousarray(a.T)
            elif callable(tr):
                a = np.ascontiguousarray(tr(a))
            if a.ndim == 0:
                a = a.reshape(1)
            wmap["w_" + kk] = a
            wshapes[kk] = a.shape
        nc = _build_block(builder, wshapes, needs_rank)
        in_maps = []
        for ci in range(n_cores):
            m = dict(wmap)
            m["xin"] = cur[ci]
            if needs_rank:
                m["rankmask"] = rmasks[ci]
            in_maps.append(m)
        res = _run_spmd(nc, in_maps, core_ids=list(range(n_cores)))
        cur = [np.ascontiguousarray(np.asarray(res.results[ci]["xout"], dtype=np.float32))
               for ci in range(n_cores)]
    out = np.empty((B_, N_, D_), dtype=np.float32)
    for ci in range(n_cores):
        out[per * ci:per * ci + per] = cur[ci].reshape(per, N_, D_)
    return out
